# version 15
# baseline (speedup 1.0000x reference)
import numpy as np
from contextlib import ExitStack

import concourse.bass as bass
import concourse.mybir as mybir
from concourse.bass_utils import run_bass_kernel_spmd

F32 = mybir.dt.float32
BF16 = mybir.dt.bfloat16
AF = mybir.ActivationFunctionType
ALU = mybir.AluOpType
AX = mybir.AxisListType

W = 8
T = 8192
TOK = T // W
D = 2048
NFC = D // 128
DI = 4096
DIC = DI // W
NG = DIC // 128
DS = 16
DTR = 128
HH = 4096
TC = 256
NCH = T // TC
NU = NCH * NG
RG = [list(range(W))]
EPS = 1e-5
RSQRT8 = float(1.0 / np.sqrt(8.0))


class Sem:

    def __init__(self, handle):
        self.h = handle
        self.c = 0

    def inc(self, instr, k=1):
        instr.then_inc(self.h, k)
        self.c += k
        return self.c


def mm512(tensor, out, lhsT, rhs, start, stop, n=TOK):
    ins = None
    for c0 in range(0, n, 512):
        ins = tensor.matmul(out=out[:, c0:c0 + 512], lhsT=lhsT,
                            rhs=rhs[:, c0:c0 + 512], start=start, stop=stop)
    return ins


def build_nc():
    nc = bass.Bass("TRN2", target_bir_lowering=False)

    def inp(name, shape):
        return nc.dram_tensor(name, list(shape), F32, kind="ExternalInput").ap()

    xT = inp("xT", [D, TOK])
    ln1w = inp("ln1w", [128, NFC]); ln1b = inp("ln1b", [128, NFC])
    ln2w = inp("ln2w", [128, NFC]); ln2b = inp("ln2b", [128, NFC])
    fbias = inp("fbias", [128, NFC])
    wInXT = inp("wInXT", [D, DIC])
    wInZT = inp("wInZT", [D, DIC])
    convw = inp("convw", [128, 4 * NG])
    convb = inp("convb", [128, NG])
    dskip = inp("dskip", [128, NG])
    dtb = inp("dtb", [128, NG])
    alog = inp("alog", [128, DS * NG])
    xpT = inp("xpT", [128, 160 * NG])
    dtpT = inp("dtpT", [DTR, DIC])
    outpT = inp("outpT", [DI, D])
    ffpT = inp("ffpT", [D, 2 * HH])
    ffoT = inp("ffoT", [HH, D])
    fct = inp("fct", [8 * D, D // W])
    consts = inp("consts", [128, 4])

    out_fm = nc.dram_tensor("out_fm", [D, TOK], F32, kind="ExternalOutput").ap()

    def idram(name, shape, dt, shared=False):
        kw = dict(addr_space="Shared") if shared else {}
        return nc.dram_tensor(name, list(shape), dt, **kw).ap()

    a1b_own = idram("a1b_own", [D, TOK], BF16)
    a1b_all = idram("a1b_all", [W * D, TOK], BF16, shared=True)
    xdbl_part = idram("xdbl_part", [160, T], F32)
    xdbl = idram("xdbl", [160, T], F32, shared=True)
    xc_sp = idram("xc_sp", [DIC, T], F32)
    z_sp = idram("z_sp", [DIC, T], F32)
    u0_sp = idram("u0_sp", [D, TOK], F32)
    y2_own = [idram(f"y2_own{q}", [DIC, T // 4], BF16) for q in range(4)]
    y2_all = [idram(f"y2_all{q}", [DI, T // 4], BF16, shared=True)
              for q in range(4)]
    y2_loc = idram("y2_loc", [DI, TOK], BF16)
    fs_own = idram("fs_own", [D, D // W], BF16)
    fs_all = idram("fs_all", [W * D, D // W], BF16, shared=True)

    st = ExitStack()

    def sb(name, shape, dt, stk=None):
        side = "left" if stk is None else "right"
        return (stk or st).enter_context(
            nc.sbuf_tensor(name, list(shape), dt, side=side))

    def psum(name, shape, stk=None):
        return (stk or st).enter_context(nc.psum_tensor(name, list(shape), F32))

    _sem_i = [0]

    def sem():
        _sem_i[0] += 1
        return Sem(st.enter_context(nc.semaphore(f"s{_sem_i[0]}")))

    ones = sb("ones", [128, 128], F32)
    ln1w_s = sb("ln1w_s", [128, NFC], F32); ln1b_s = sb("ln1b_s", [128, NFC], F32)
    ln2w_s = sb("ln2w_s", [128, NFC], F32); ln2b_s = sb("ln2b_s", [128, NFC], F32)
    fbias_s = sb("fbias_s", [128, NFC], F32)
    convw_s = sb("convw_s", [128, 4 * NG], F32)
    convb_s = sb("convb_s", [128, NG], F32)
    dskip_s = sb("dskip_s", [128, NG], F32)
    dtb_s = sb("dtb_s", [128, NG], F32)
    a_s = sb("a_s", [128, DS * NG], F32)
    xp_bf = sb("xp_bf", [128, 160 * NG], BF16)
    dtp_bf = sb("dtp_bf", [DTR, DIC], BF16)
    consts_s = sb("consts_s", [128, 4], F32)
    for i, val in enumerate((EPS, 1.0, -1.0, RSQRT8)):
        nc.const_aps.register(F32, float(val), consts_s[:, i:i + 1])

    ph = sem()
    cc = sem()
    tn = sem()
    cv = sem()
    cs = sem()
    so = sem()
    ld = sem()

    ENG = ("sync", "vector", "scalar", "tensor", "gpsimd")
    bh = {}

    def barrier(extra=()):
        blk = bh["blk"]
        extra = [(s_, s_.c) for s_ in extra]
        target = ph.c + len(ENG)
        ph.c = target
        for e in ENG:
            def body(eng):
                eng.drain()
                eng.sem_inc(ph.h, 1)
                eng.wait_ge(ph.h, target)
                for s_, v_ in extra:
                    eng.wait_ge(s_.h, v_)
            getattr(blk, e)(body)

    def emit_ln(src_fn, dst_fn, w_s, b_s, lnp, scr, pre=()):
        blk = bh["blk"]
        pst, psr, psq = lnp
        sq0, sq1, m_row, r_row, q_row, t1 = scr
        sqs = (sq0, sq1)
        cs0, tn0 = cs.c, tn.c

        @blk.scalar
        def _(scalar):
            for s_, v_ in pre:
                scalar.wait_ge(s_.h, v_)
            for fc in range(NFC):
                if fc >= 2:
                    scalar.wait_ge(tn.h, tn0 + fc - 1)
                cs.inc(scalar.activation(out=sqs[fc % 2][:], in_=src_fn(fc),
                                         func=AF.Square))

        @blk.tensor
        def _(tensor):
            for s_, v_ in pre:
                tensor.wait_ge(s_.h, v_)
            for fc in range(NFC):
                for hf in range(2):
                    sl = slice(hf * 512, (hf + 1) * 512)
                    tensor.matmul(out=pst[0:1, sl], lhsT=ones[:, 0:1],
                                  rhs=src_fn(fc)[:, sl],
                                  start=(fc == 0), stop=(fc == NFC - 1))
                tensor.wait_ge(cs.h, cs0 + fc + 1)
                ins = None
                for hf in range(2):
                    sl = slice(hf * 512, (hf + 1) * 512)
                    ins = tensor.matmul(out=pst[32:33, sl], lhsT=ones[:, 0:1],
                                        rhs=sqs[fc % 2][:, sl],
                                        start=(fc == 0), stop=(fc == NFC - 1))
                tn.inc(ins)

        cv0 = cv.c

        @blk.vector
        def _(vector):
            vector.wait_ge(tn.h, tn.c)
            vector.tensor_scalar(out=m_row[:], in0=pst[0:1, :], scalar1=1.0 / D,
                                 scalar2=None, op0=ALU.mult).then_inc(cv.h, 1)
            vector.wait_ge(cv.h, cv0 + 1)
            vector.tensor_scalar(out=r_row[:], in0=pst[32:33, :], scalar1=1.0 / D,
                                 scalar2=None, op0=ALU.mult).then_inc(cv.h, 1)
            vector.wait_ge(cv.h, cv0 + 2)
            vector.tensor_mul(out=q_row[:], in0=m_row[:],
                              in1=m_row[:]).then_inc(cv.h, 1)
            vector.wait_ge(cv.h, cv0 + 3)
            vector.tensor_sub(out=r_row[:], in0=r_row[:],
                              in1=q_row[:]).then_inc(cv.h, 1)
        cv.c += 4

        @blk.scalar
        def _(scalar):
            scalar.wait_ge(cv.h, cv.c)
            cs.inc(scalar.activation(out=q_row[:], in_=r_row[:], func=AF.Sqrt,
                                     bias=EPS))

        cv1 = cv.c

        @blk.vector
        def _(vector):
            vector.wait_ge(cs.h, cs.c)
            vector.reciprocal(out=r_row[:], in_=q_row[:]).then_inc(cv.h, 1)
            vector.wait_ge(cv.h, cv1 + 1)
            vector.tensor_mul(out=q_row[:], in0=m_row[:],
                              in1=r_row[:]).then_inc(cv.h, 1)
            vector.wait_ge(cv.h, cv1 + 2)
            vector.tensor_scalar(out=q_row[:], in0=q_row[:], scalar1=-1.0,
                                 scalar2=None, op0=ALU.mult).then_inc(cv.h, 1)
        cv.c += 3

        @blk.tensor
        def _(tensor):
            tensor.wait_ge(cv.h, cv.c)
            for half in range(2):
                sl = slice(half * 512, (half + 1) * 512)
                tensor.matmul(out=psr[:, sl], lhsT=ones[0:1, :], rhs=r_row[:, sl],
                              start=True, stop=True)
                tn.inc(tensor.matmul(out=psq[:, sl], lhsT=ones[0:1, :],
                                     rhs=q_row[:, sl], start=True, stop=True))

        cv2 = cv.c

        @blk.vector
        def _(vector):
            vector.wait_ge(tn.h, tn.c)
            for fc in range(NFC):
                vector.tensor_mul(out=t1[:], in0=src_fn(fc),
                                  in1=psr[:]).then_inc(cv.h, 1)
                vector.wait_ge(cv.h, cv2 + 3 * fc + 1)
                if w_s is None:
                    vector.tensor_add(out=dst_fn(fc), in0=t1[:],
                                      in1=psq[:]).then_inc(cv.h, 2)
                    vector.wait_ge(cv.h, cv2 + 3 * fc + 3)
                else:
                    vector.tensor_add(out=t1[:], in0=t1[:],
                                      in1=psq[:]).then_inc(cv.h, 1)
                    vector.wait_ge(cv.h, cv2 + 3 * fc + 2)
                    vector.tensor_scalar(
                        out=dst_fn(fc), in0=t1[:],
                        scalar1=w_s[:, fc:fc + 1], scalar2=b_s[:, fc:fc + 1],
                        op0=ALU.mult, op1=ALU.add).then_inc(cv.h, 1)
                    vector.wait_ge(cv.h, cv2 + 3 * fc + 3)
        cv.c += 3 * NFC

    with nc.Block() as blk:
        bh["blk"] = blk

        stA = ExitStack()
        stAB = ExitStack()
        stABC = ExitStack()
        wInX_bf = sb("wInX_bf", [128, NFC * 512], BF16, stABC)
        wInZ_bf = sb("wInZ_bf", [128, NFC * 512], BF16, stABC)
        x_sb = sb("x_sb", [128, NFC * TOK], F32, stAB)
        wstg = sb("wstg", [128, NFC * 512], F32, stA)
        wstgz = sb("wstgz", [128, NFC * 256], F32, stA)
        fstg = sb("fstg", [128, NFC * 256], F32, stA)
        facc = sb("facc", [128, NFC * 256], F32, stA)
        fbf = sb("fbf", [128, NFC * 256], BF16, stA)
        xp_st = sb("xp_st", [128, 160 * NG], F32, stA)
        dtp_st = sb("dtp_st", [DTR, DIC], F32, stA)
        ascr = sb("ascr", [128, DS * NG], F32, stA)

        s_par = sem()
        s_wx = sem()
        s_wz = sem()
        s_fc = [sem() for _ in range(2)]
        s_fs = sem()

        @blk.sync
        def _(sync):
            s_par.inc(sync.dma_start(
                out=x_sb[:].rearrange("p (f t) -> p f t", t=TOK),
                in_=xT.rearrange("(f p) t -> p f t", p=128)), 16)
            for t_, d_ in ((ln1w, ln1w_s), (ln1b, ln1b_s), (ln2w, ln2w_s),
                           (ln2b, ln2b_s), (fbias, fbias_s), (convw, convw_s),
                           (convb, convb_s), (dskip, dskip_s), (dtb, dtb_s),
                           (alog, a_s), (xpT, xp_st), (dtpT, dtp_st),
                           (consts, consts_s)):
                s_par.inc(sync.dma_start(out=d_[:], in_=t_), 16)
            s_wx.inc(sync.dma_start(
                out=wstg[:].rearrange("p (k m) -> p k m", m=512),
                in_=wInXT.rearrange("(k p) m -> p k m", p=128)), 16)

        cvA0 = cv.c

        @blk.vector
        def _(vector):
            vector.memset(ones[:], 1.0).then_inc(cv.h, 1)
            vector.wait_ge(s_par.h, s_par.c)
            vector.tensor_copy(out=xp_bf[:], in_=xp_st[:]).then_inc(cv.h, 1)
            vector.tensor_copy(out=dtp_bf[:], in_=dtp_st[:]).then_inc(cv.h, 1)
            vector.wait_ge(s_wx.h, s_wx.c)
            vector.tensor_copy(out=wInX_bf[:], in_=wstg[:]).then_inc(cv.h, 1)
        cv.c += 4

        s_wzc = sem()

        @blk.sync
        def _(sync):
            for h_ in range(2):
                if h_ >= 1:
                    sync.wait_ge(s_wzc.h, h_)
                s_wz.inc(sync.dma_start(
                    out=wstgz[:].rearrange("p (k m) -> p k m", m=256),
                    in_=wInZT[:, h_ * 256:(h_ + 1) * 256].rearrange(
                        "(k p) m -> p k m", p=128)), 16)

        @blk.vector
        def _(vector):
            for h_ in range(2):
                vector.wait_ge(s_wz.h, 16 * (h_ + 1))
                ww = wInZ_bf[:].rearrange("p (k m) -> p k m", m=512)
                vector.tensor_copy(
                    out=ww[:, :, h_ * 256:(h_ + 1) * 256],
                    in_=wstgz[:].rearrange("p (k m) -> p k m", m=256)
                    ).then_inc(s_wzc.h, 1)
        s_wzc.c += 2

        @blk.scalar
        def _(scalar):
            scalar.wait_ge(s_par.h, s_par.c)
            cs.inc(scalar.activation(out=ascr[:], in_=a_s[:], func=AF.Exp))

        @blk.vector
        def _(vector):
            vector.wait_ge(cs.h, cs.c)
            vector.tensor_scalar(out=a_s[:], in0=ascr[:], scalar1=-1.0,
                                 scalar2=None, op0=ALU.mult).then_inc(cv.h, 1)
        cv.c += 1

        cvF0 = cv.c
        fc_ms = {}
        _cnt = [0, 0]
        for r in range(8):
            _cnt[r % 2] += 16
            fc_ms[r] = (r % 2, _cnt[r % 2])

        @blk.gpsimd
        def _(gpsimd):
            for r in range(8):
                if r >= 2:
                    gpsimd.wait_ge(cv.h, cvF0 + (r - 1))
                dst = facc if r == 0 else fstg
                s_fc[r % 2].inc(gpsimd.dma_start(
                    out=dst[:].rearrange("p (f e) -> p f e", e=256),
                    in_=fct[r * D:(r + 1) * D, :].rearrange(
                        "(f p) e -> p f e", p=128)), 16)
                if r >= 1:
                    p_, v_ = fc_ms[r]
                    gpsimd.wait_ge(s_fc[p_].h, v_)
                    if r == 1:
                        gpsimd.wait_ge(s_fc[0].h, 16)
                    gpsimd.tensor_add(out=facc[:], in0=facc[:],
                                      in1=fstg[:]).then_inc(cv.h, 1)
            gpsimd.wait_ge(cv.h, cvF0 + 7)
            gpsimd.tensor_copy(out=fbf[:], in_=facc[:]).then_inc(cv.h, 1)
        cv.c += 8

        @blk.sync
        def _(sync):
            sync.wait_ge(cv.h, cv.c)
            s_fs.inc(sync.dma_start(
                out=fs_own.rearrange("(f p) e -> p f e", p=128),
                in_=fbf[:].rearrange("p (f e) -> p f e", e=256)), 16)

        @blk.gpsimd
        def _(gpsimd):
            gpsimd.wait_ge(s_fs.h, s_fs.c)
            cc.inc(gpsimd.collective_compute(
                "AllGather", ALU.bypass, replica_groups=RG,
                ins=[fs_own.opt()], outs=[fs_all.opt()]))
        CC_FS = cc.c

        barrier(extra=(s_fs,))
        stA.close()

        stB = ExitStack()
        u0_sb = x_sb
        a1b_sb = sb("a1b_sb", [128, NFC * TOK], BF16, stB)
        pstB = psum("pstB", [128, TOK], stB)
        psrB = psum("psrB", [128, TOK], stB)
        psqB = psum("psqB", [128, TOK], stB)
        scrB = (sb("sq0", [128, TOK], F32, stB), sb("sq1", [128, TOK], F32, stB),
                sb("m_row", [1, TOK], F32, stB), sb("r_row", [1, TOK], F32, stB),
                sb("q_row", [1, TOK], F32, stB), sb("t1B", [128, TOK], F32, stB))

        emit_ln(lambda fc: x_sb[:, fc * TOK:(fc + 1) * TOK],
                lambda fc: u0_sb[:, fc * TOK:(fc + 1) * TOK],
                None, None, (pstB, psrB, psqB), scrB,
                pre=((s_par, s_par.c),))
        emit_ln(lambda fc: u0_sb[:, fc * TOK:(fc + 1) * TOK],
                lambda fc: a1b_sb[:, fc * TOK:(fc + 1) * TOK],
                ln1w_s, ln1b_s, (pstB, psrB, psqB), scrB,
                pre=((cv, cv.c), (s_par, s_par.c)))

        s_u0sp = sem()
        s_a1 = sem()

        @blk.sync
        def _(sync):
            sync.wait_ge(cv.h, cv.c)
            s_u0sp.inc(sync.dma_start(
                out=u0_sp.rearrange("(f p) t -> p f t", p=128),
                in_=u0_sb[:].rearrange("p (f t) -> p f t", t=TOK)), 16)
            s_a1.inc(sync.dma_start(
                out=a1b_own.rearrange("(f p) t -> p f t", p=128),
                in_=a1b_sb[:].rearrange("p (f t) -> p f t", t=TOK)), 16)

        @blk.gpsimd
        def _(gpsimd):
            gpsimd.wait_ge(s_a1.h, s_a1.c)
            cc.inc(gpsimd.collective_compute(
                "AllGather", ALU.bypass, replica_groups=RG,
                ins=[a1b_own.opt()], outs=[a1b_all.opt()]))
        CC_A1 = cc.c

        barrier(extra=(s_u0sp, s_a1))
        stB.close()
        stAB.close()

        stC = ExitStack()
        wInX_bf2 = wInX_bf
        wInZ_bf2 = wInZ_bf
        rhs_t = [sb(f"rhs{i}", [128, NFC * TOK], BF16, stC) for i in range(2)]
        ps_in = [psum(f"ps_in{i}", [128, TOK], stC) for i in range(2)]
        ps_xdt = psum("ps_xdt", [128, TOK], stC)
        ps_xbc = psum("ps_xbc", [32, TOK], stC)
        xm_t = [sb(f"xm{g}", [128, TOK + 3], F32, stC) for g in range(NG)]
        xc_t = [sb(f"xct{g}", [128, TOK], F32, stC) for g in range(NG)]
        xcb_t = [sb(f"xcb{g}", [128, TOK], BF16, stC) for g in range(NG)]
        z_t = [sb(f"zt{g}", [128, TOK], F32, stC) for g in range(NG)]
        xdt_ev = sb("xdt_ev", [128, TOK], F32, stC)
        xbc_ev = sb("xbc_ev", [32, TOK], F32, stC)

        s_rhs = [sem() for _ in range(2)]
        s_xcz = sem()
        s_xdb = sem()

        tc0_tn = tn.c
        tc0_cs = cs.c
        tc0_cv = cv.c

        def ev_ms(u):
            return tc0_cs + (u // 8) * 12 + (u % 8) + 1

        @blk.vector
        def _(vector):
            for g in range(NG):
                vector.memset(xm_t[g][:, 0:3], 0.0).then_inc(cv.h, 1)
        cv.c += NG
        tc0_cv = cv.c

        @blk.sync
        def _(sync):
            sync.wait_ge(cc.h, CC_A1)
            for tc in range(W):
                base = tc * D
                if tc >= 2:
                    sync.wait_ge(tn.h, tc0_tn + (tc - 2) * 10 + 8)
                for k in range(NFC):
                    s_rhs[tc % 2].inc(sync.dma_start(
                        out=rhs_t[tc % 2][:, k * TOK:(k + 1) * TOK],
                        in_=a1b_all[base + k * 128: base + (k + 1) * 128, :]), 16)

        @blk.tensor
        def _(tensor):
            for tc in range(W):
                tensor.wait_ge(s_rhs[tc % 2].h, 16 * NFC * (tc // 2 + 1))
                for mg in range(8):
                    u = tc * 8 + mg
                    if u >= 2:
                        tensor.wait_ge(cs.h, ev_ms(u - 2))
                    wsel = wInX_bf2 if mg < 4 else wInZ_bf2
                    msub = mg % 4
                    ins = None
                    for k in range(NFC):
                        ins = mm512(tensor, ps_in[u % 2],
                                    wsel[:, k * 512 + msub * 128:
                                         k * 512 + (msub + 1) * 128],
                                    rhs_t[tc % 2][:, k * TOK:(k + 1) * TOK],
                                    start=(k == 0), stop=(k == NFC - 1))
                    tn.inc(ins)
                tensor.wait_ge(cv.h, tc0_cv + tc * 26 + 24)
                if tc >= 1:
                    tensor.wait_ge(cv.h, tc0_cv + (tc - 1) * 26 + 26)
                ins = None
                for g in range(NG):
                    mm512(tensor, ps_xdt, xp_bf[:, g * 160:g * 160 + 128],
                          xcb_t[g][:], start=(g == 0), stop=(g == NG - 1))
                    ins = mm512(tensor, ps_xbc,
                                xp_bf[:, g * 160 + 128:g * 160 + 160],
                                xcb_t[g][:], start=(g == 0), stop=(g == NG - 1))
                tn.inc(ins, 2)

        @blk.scalar
        def _(scalar):
            for tc in range(W):
                for mg in range(8):
                    scalar.wait_ge(tn.h, tc0_tn + tc * 10 + mg + 1)
                    if mg < 4:
                        if tc >= 1:
                            scalar.wait_ge(cv.h,
                                           tc0_cv + (tc - 1) * 26 + mg * 5 + 5)
                        cs.inc(scalar.copy(out=xm_t[mg][:, 3:3 + TOK],
                                           in_=ps_in[(tc * 8 + mg) % 2][:]))
                    else:
                        if tc >= 1:
                            scalar.wait_ge(s_xcz.h, 128 * tc)
                        cs.inc(scalar.activation(
                            out=z_t[mg - 4][:], in_=ps_in[(tc * 8 + mg) % 2][:],
                            func=AF.Silu))
                for g in range(NG):
                    scalar.wait_ge(cv.h, tc0_cv + tc * 26 + g * 5 + 4)
                    cs.inc(scalar.activation(out=xc_t[g][:], in_=xc_t[g][:],
                                             func=AF.Silu,
                                             bias=convb_s[:, g:g + 1]))

        @blk.vector
        def _(vector):
            vector.wait_ge(s_par.h, s_par.c)
            for tc in range(W):
                cvb = tc0_cv + tc * 26
                for g in range(NG):
                    vector.wait_ge(cs.h, tc0_cs + tc * 12 + g + 1)
                    if tc >= 1:
                        vector.wait_ge(s_xcz.h, 128 * tc)
                        vector.wait_ge(cv.h, tc0_cv + (tc - 1) * 26 + g * 5 + 5)
                    else:
                        vector.wait_ge(cv.h, tc0_cv)
                    vector.tensor_scalar(
                        out=xc_t[g][:], in0=xm_t[g][:, 0:TOK],
                        scalar1=convw_s[:, g * 4:g * 4 + 1], scalar2=None,
                        op0=ALU.mult).then_inc(cv.h, 1)
                    for k in range(1, 4):
                        vector.wait_ge(cv.h, cvb + g * 5 + k)
                        vector.scalar_tensor_tensor(
                            out=xc_t[g][:], in0=xm_t[g][:, k:k + TOK],
                            scalar=convw_s[:, g * 4 + k:g * 4 + k + 1],
                            in1=xc_t[g][:], op0=ALU.mult,
                            op1=ALU.add).then_inc(cv.h, 1)
                    vector.wait_ge(cv.h, cvb + g * 5 + 4)
                    vector.tensor_copy(out=xm_t[g][:, 0:3],
                                       in_=xm_t[g][:, TOK:TOK + 3]
                                       ).then_inc(cv.h, 1)
                for g in range(NG):
                    vector.wait_ge(cs.h, tc0_cs + tc * 12 + 8 + g + 1)
                    vector.tensor_copy(out=xcb_t[g][:],
                                       in_=xc_t[g][:]).then_inc(cv.h, 1)
                vector.wait_ge(tn.h, tc0_tn + tc * 10 + 10)
                if tc >= 1:
                    vector.wait_ge(s_xdb.h, 32 * tc)
                vector.tensor_copy(out=xdt_ev[:], in_=ps_xdt[:]).then_inc(cv.h, 1)
                vector.tensor_copy(out=xbc_ev[:], in_=ps_xbc[:]).then_inc(cv.h, 1)
        cv.c += 26 * W

        @blk.gpsimd
        def _(gpsimd):
            for tc in range(W):
                for g in range(NG):
                    gpsimd.wait_ge(cs.h, tc0_cs + tc * 12 + 8 + g + 1)
                    s_xcz.inc(gpsimd.dma_start(
                        out=xc_sp[g * 128:(g + 1) * 128,
                                  tc * TOK:(tc + 1) * TOK],
                        in_=xc_t[g][:]), 16)
                    gpsimd.wait_ge(cs.h, tc0_cs + tc * 12 + 4 + g + 1)
                    s_xcz.inc(gpsimd.dma_start(
                        out=z_sp[g * 128:(g + 1) * 128,
                                 tc * TOK:(tc + 1) * TOK],
                        in_=z_t[g][:]), 16)
                gpsimd.wait_ge(cv.h, tc0_cv + tc * 26 + 26)
                s_xdb.inc(gpsimd.dma_start(
                    out=xdbl_part[0:128, tc * TOK:(tc + 1) * TOK],
                    in_=xdt_ev[:]), 16)
                s_xdb.inc(gpsimd.dma_start(
                    out=xdbl_part[128:160, tc * TOK:(tc + 1) * TOK],
                    in_=xbc_ev[:]), 16)

        @blk.gpsimd
        def _(gpsimd):
            gpsimd.wait_ge(s_xdb.h, s_xdb.c)
            cc.inc(gpsimd.collective_compute(
                "AllReduce", ALU.add, replica_groups=RG,
                ins=[xdbl_part.opt()], outs=[xdbl.opt()]))
        CC_XD = cc.c

        barrier(extra=(s_xcz, s_xdb))
        stC.close()
        stABC.close()

        stD = ExitStack()
        dtraw_bf = sb("dtraw_bf", [DTR, T], BF16, stD)
        bt_t = [sb(f"bt{i}", [128, DS * TC], F32, stD) for i in range(2)]
        ct_t = [sb(f"ct{i}", [128, DS * TC], F32, stD) for i in range(2)]
        g_t = [sb(f"gt{i}", [128, DS * TC], F32, stD) for i in range(3)]
        u_sb = [sb(f"u_sb{i}", [128, DS * TC], F32, stD) for i in range(2)]
        h_sb = sb("h_sb", [128, DS * TC], F32, stD)
        hst = sb("hst", [128, DS * NG], F32, stD)
        tmp16 = sb("tmp16", [128, DS], F32, stD)
        dt_t = [sb(f"dt{i}", [128, TC], F32, stD) for i in range(3)]
        w2_t = [sb(f"w2_{i}", [128, 2 * TC], F32, stD) for i in range(3)]
        tabs = sb("tabs", [128, TC], F32, stD)
        texp = sb("texp", [128, TC], F32, stD)
        tln = [sb(f"tln{i}", [128, TC], F32, stD) for i in range(2)]
        trelu = [sb(f"trelu{i}", [128, TC], F32, stD) for i in range(2)]
        xcg_t = [sb(f"xcg{i}", [128, TC], F32, stD) for i in range(4)]
        zg_t = [sb(f"zg{i}", [128, TC], F32, stD) for i in range(4)]
        y_sb = sb("y_sb", [128, TC], F32, stD)
        y2b_t = [sb(f"y2b{i}", [128, TC], BF16, stD) for i in range(4)]
        ps_dt = [psum(f"ps_dt{i}", [128, TC], stD) for i in range(3)]

        s_dst = [sem() for _ in range(2)]
        s_xcg = [sem() for _ in range(4)]
        s_zg = [sem() for _ in range(4)]
        s_bc = [sem() for _ in range(2)]
        s_y2 = [sem() for _ in range(4)]

        pp = sem()
        d0_cv = cv.c
        d0_cs = cs.c
        d0_tn = tn.c
        VI = 17
        PI = 1
        SI = 5
        SPL = 10

        @blk.sync
        def _(sync):
            sync.wait_ge(cc.h, CC_XD)
            for j in range(4):
                if j >= 2:
                    sync.wait_ge(cv.h, d0_cv + (j - 2) + 1)
                s_dst[j % 2].inc(sync.dma_start(
                    out=bt_t[0][:, (j % 2) * 2048:(j % 2 + 1) * 2048],
                    in_=xdbl[0:128, j * 2048:(j + 1) * 2048]), 16)

        @blk.vector
        def _(vector):
            for j in range(4):
                vector.wait_ge(s_dst[j % 2].h, 16 * (j // 2 + 1))
                vector.tensor_copy(out=dtraw_bf[:, j * 2048:(j + 1) * 2048],
                                   in_=bt_t[0][:, (j % 2) * 2048:(j % 2 + 1) * 2048]
                                   ).then_inc(cv.h, 1)
            vector.wait_ge(cv.h, d0_cv + 4)
            vector.memset(hst[:], 0.0).then_inc(cv.h, 1)
        cv.c += 5

        d1_cv = cv.c

        HEAD = ("dtadd", "wm", "wc")
        TAIL = ("ub", "foldm", "folda", "memset", "scan", "hst", "rmult",
                "tree1", "tree2", "tree3", "comb", "yfold", "stt", "y2b")
        sched = []
        for u in range(NU):
            sched.extend((op, u) for op in HEAD)
            if u >= 2:
                sched.extend((op, u - 2) for op in TAIL)
        sched.extend((op, NU - 2) for op in TAIL)
        sched.extend((op, NU - 1) for op in TAIL)
        L = {item: d1_cv + i + 1 for i, item in enumerate(sched)}

        def P(op, u):
            return u + 1

        @blk.sync
        def _(sync):
            sync.wait_ge(cc.h, CC_XD)
            for ch in range(NCH):
                c0 = ch * TC
                if ch == 0:
                    sync.wait_ge(cv.h, d0_cv + 4)
                if ch >= 2:
                    lastu = NG * (ch - 2) + (NG - 1)
                    sync.wait_ge(cv.h, L[("rmult", lastu)])
                for s in range(DS):
                    s_bc[ch % 2].inc(sync.dma_start(
                        out=bt_t[ch % 2][:, s * TC:(s + 1) * TC],
                        in_=xdbl[128 + s:129 + s, c0:c0 + TC]
                        .partition_broadcast(128)), 16)
                    s_bc[ch % 2].inc(sync.dma_start(
                        out=ct_t[ch % 2][:, s * TC:(s + 1) * TC],
                        in_=xdbl[144 + s:145 + s, c0:c0 + TC]
                        .partition_broadcast(128)), 16)
                for g in range(NG):
                    u = ch * NG + g
                    if u >= 4:
                        sync.wait_ge(cv.h, L[("stt", u - 4)])
                    s_xcg[u % 4].inc(sync.dma_start(
                        out=xcg_t[u % 4][:],
                        in_=xc_sp[g * 128:(g + 1) * 128,
                                  ch * TC:(ch + 1) * TC]), 16)
                    if u >= 4:
                        sync.wait_ge(cv.h, L[("y2b", u - 4)])
                    s_zg[u % 4].inc(sync.dma_start(
                        out=zg_t[u % 4][:],
                        in_=z_sp[g * 128:(g + 1) * 128,
                                 ch * TC:(ch + 1) * TC]), 16)

        @blk.gpsimd
        def _(gpsimd):
            gpsimd.wait_ge(cc.h, CC_XD)
            for ch in range(NCH):
                for g in range(NG):
                    u = ch * NG + g
                    if ch >= 1:
                        up = (ch - 1) * NG + g
                        qq, qc = (ch - 1) // 8, (ch - 1) % 8
                        gpsimd.wait_ge(cv.h, L[("y2b", up)])
                        s_y2[g].inc(gpsimd.dma_start(
                            out=y2_own[qq][g * 128:(g + 1) * 128,
                                          qc * TC:(qc + 1) * TC],
                            in_=y2b_t[g][:]), 16)
        pp.c += 0

        @blk.gpsimd
        def _(gpsimd):
            for g in range(NG):
                up = (NCH - 1) * NG + g
                gpsimd.wait_ge(cv.h, L[("y2b", up)])
                s_y2[g].inc(gpsimd.dma_start(
                    out=y2_own[3][g * 128:(g + 1) * 128, 7 * TC:],
                    in_=y2b_t[g][:]), 16)

        @blk.tensor
        def _(tensor):
            tensor.wait_ge(cv.h, d0_cv + 4)
            for ch in range(NCH):
                for g in range(NG):
                    u = ch * NG + g
                    if u >= 3:
                        tensor.wait_ge(cs.h, d0_cs + SI * (u - 3) + 4)
                    tn.inc(tensor.matmul(out=ps_dt[u % 3][:],
                                         lhsT=dtp_bf[:, g * 128:(g + 1) * 128],
                                         rhs=dtraw_bf[:, ch * TC:(ch + 1) * TC],
                                         start=True, stop=True))

        @blk.scalar
        def _(scalar):
            scalar.wait_ge(s_par.h, s_par.c)
            for ch in range(NCH):
                for g in range(NG):
                    u = ch * NG + g
                    csb = d0_cs + SI * u
                    scalar.wait_ge(tn.h, d0_tn + u + 1)
                    scalar.activation(out=tabs[:], in_=ps_dt[u % 3][:],
                                      func=AF.Abs,
                                      bias=dtb_s[:, g:g + 1]).then_inc(cs.h, 1)
                    scalar.wait_ge(cs.h, csb + 1)
                    scalar.activation(out=texp[:], in_=tabs[:], func=AF.Exp,
                                      scale=-1.0).then_inc(cs.h, 1)
                    scalar.wait_ge(cs.h, csb + 2)
                    if u >= 2:
                        scalar.wait_ge(cv.h, L[("dtadd", u - 2)])
                    scalar.activation(out=tln[u % 2][:], in_=texp[:], func=AF.Ln,
                                      bias=1.0).then_inc(cs.h, 1)
                    scalar.activation(out=trelu[u % 2][:], in_=ps_dt[u % 3][:],
                                      func=AF.Relu,
                                      bias=dtb_s[:, g:g + 1]).then_inc(cs.h, 1)
                    scalar.wait_ge(cv.h, L[("dtadd", u)])
                    if u >= 3:
                        scalar.wait_ge(cv.h, L[("scan", u - 3)])
                    ins = None
                    for s in range(DS):
                        ins = scalar.activation(
                            out=g_t[u % 3][:, s * TC:(s + 1) * TC],
                            in_=dt_t[u % 3][:], func=AF.Exp,
                            scale=a_s[:, g * DS + s:g * DS + s + 1])
                    ins.then_inc(cs.h, 1)
        cs.c += SI * NU

        def emit_head(vector, u):
            csb = d0_cs + SI * u
            vector.wait_ge(cs.h, csb + 4)
            if u >= 3:
                vector.wait_ge(cs.h, d0_cs + SI * (u - 3) + 5)
                vector.wait_ge(cv.h, L[("wc", u - 3)])
            vector.tensor_add(out=dt_t[u % 3][:], in0=tln[u % 2][:],
                              in1=trelu[u % 2][:]).then_inc(cv.h, 1)
            vector.wait_ge(s_xcg[u % 4].h, 16 * (u // 4 + 1))
            vector.wait_ge(cv.h, L[("dtadd", u)])
            if u >= 3:
                vector.wait_ge(cv.h, L[("ub", u - 3)])
            vector.tensor_mul(out=w2_t[u % 3][:, 0:TC], in0=dt_t[u % 3][:],
                              in1=xcg_t[u % 4][:]).then_inc(cv.h, 1)
            vector.wait_ge(cv.h, L[("wm", u)])
            vector.tensor_copy(out=w2_t[u % 3][:, TC:2 * TC],
                               in_=w2_t[u % 3][:, 0:TC]).then_inc(cv.h, 1)

        def emit_tail(vector, u):
            ch, g = u // NG, u % NG
            csb = d0_cs + SI * u
            vector.wait_ge(s_bc[ch % 2].h, 512 * (ch // 2 + 1))
            vector.wait_ge(cv.h, L[("wc", u)])
            ins = None
            for s2 in range(0, DS, 2):
                ins = vector.tensor_mul(
                    out=u_sb[u % 2][:, s2 * TC:(s2 + 2) * TC],
                    in0=w2_t[u % 3][:],
                    in1=bt_t[ch % 2][:, s2 * TC:(s2 + 2) * TC])
            ins.then_inc(cv.h, 1)
            vector.wait_ge(cs.h, csb + 5)
            vector.wait_ge(cv.h, L[("ub", u)])
            if u >= 4:
                vector.wait_ge(cv.h, L[("hst", u - 4)])
            gst = g_t[u % 3][:, 0::TC]
            ust = u_sb[u % 2][:, 0::TC]
            vector.tensor_mul(out=tmp16[:], in0=gst,
                              in1=hst[:, g * DS:(g + 1) * DS]).then_inc(cv.h, 1)
            vector.wait_ge(cv.h, L[("foldm", u)])
            vector.tensor_add(out=ust, in0=ust, in1=tmp16[:]).then_inc(cv.h, 1)
            vector.wait_ge(cv.h, L[("folda", u)])
            vector.memset(gst, 0.0).then_inc(cv.h, 1)
            vector.wait_ge(cv.h, L[("memset", u)])
            vector.tensor_tensor_scan(
                out=h_sb[:], data0=g_t[u % 3][:], data1=u_sb[u % 2][:],
                initial=0.0, op0=ALU.mult, op1=ALU.add).then_inc(cv.h, 1)
            vector.wait_ge(cv.h, L[("scan", u)])
            vector.tensor_copy(out=hst[:, g * DS:(g + 1) * DS],
                               in_=h_sb[:, TC - 1::TC]).then_inc(cv.h, 1)
            vector.wait_ge(cv.h, L[("hst", u)])
            ins = None
            for s2 in range(0, DS, 2):
                ins = vector.tensor_mul(
                    out=h_sb[:, s2 * TC:(s2 + 2) * TC],
                    in0=h_sb[:, s2 * TC:(s2 + 2) * TC],
                    in1=ct_t[ch % 2][:, s2 * TC:(s2 + 2) * TC])
            ins.then_inc(cv.h, 1)
            vector.wait_ge(cv.h, L[("rmult", u)])
            vector.tensor_add(out=h_sb[:, 0:2 * TC], in0=h_sb[:, 0:2 * TC],
                              in1=h_sb[:, 2 * TC:4 * TC])
            vector.tensor_add(out=h_sb[:, 4 * TC:6 * TC],
                              in0=h_sb[:, 4 * TC:6 * TC],
                              in1=h_sb[:, 6 * TC:8 * TC]).then_inc(cv.h, 1)
            vector.wait_ge(cv.h, L[("tree1", u)])
            vector.tensor_add(out=h_sb[:, 8 * TC:10 * TC],
                              in0=h_sb[:, 8 * TC:10 * TC],
                              in1=h_sb[:, 10 * TC:12 * TC])
            vector.tensor_add(out=h_sb[:, 12 * TC:14 * TC],
                              in0=h_sb[:, 12 * TC:14 * TC],
                              in1=h_sb[:, 14 * TC:16 * TC]).then_inc(cv.h, 1)
            vector.wait_ge(cv.h, L[("tree2", u)])
            vector.tensor_add(out=h_sb[:, 0:2 * TC], in0=h_sb[:, 0:2 * TC],
                              in1=h_sb[:, 4 * TC:6 * TC])
            vector.tensor_add(out=h_sb[:, 8 * TC:10 * TC],
                              in0=h_sb[:, 8 * TC:10 * TC],
                              in1=h_sb[:, 12 * TC:14 * TC]).then_inc(cv.h, 1)
            vector.wait_ge(cv.h, L[("tree3", u)])
            vector.tensor_add(out=h_sb[:, 0:2 * TC], in0=h_sb[:, 0:2 * TC],
                              in1=h_sb[:, 8 * TC:10 * TC]).then_inc(cv.h, 1)
            vector.wait_ge(cv.h, L[("comb", u)])
            vector.tensor_add(out=y_sb[:], in0=h_sb[:, 0:TC],
                              in1=h_sb[:, TC:2 * TC]).then_inc(cv.h, 1)
            vector.wait_ge(cv.h, L[("yfold", u)])
            vector.scalar_tensor_tensor(
                out=y_sb[:], in0=xcg_t[u % 4][:],
                scalar=dskip_s[:, g:g + 1], in1=y_sb[:],
                op0=ALU.mult, op1=ALU.add).then_inc(cv.h, 1)
            vector.wait_ge(cv.h, L[("stt", u)])
            vector.wait_ge(s_zg[u % 4].h, 16 * (u // 4 + 1))
            if ch >= 1:
                vector.wait_ge(s_y2[g].h, 16 * ch)
            vector.tensor_mul(out=y2b_t[g][:], in0=y_sb[:],
                              in1=zg_t[u % 4][:]).then_inc(cv.h, 1)

        @blk.vector
        def _(vector):
            for u in range(NU):
                emit_head(vector, u)
                if u >= 2:
                    emit_tail(vector, u - 2)
            emit_tail(vector, NU - 2)
            emit_tail(vector, NU - 1)
        cv.c += VI * NU

        s_yl = sem()
        s_ag = [sem() for _ in range(4)]

        @blk.gpsimd
        def _(gpsimd):
            for q in range(4):
                for g in range(NG):
                    gpsimd.wait_ge(s_y2[g].h, 16 * 8 * (q + 1))
                s_ag[q].inc(gpsimd.collective_compute(
                    "AllGather", ALU.bypass, replica_groups=RG,
                    ins=[y2_own[q].opt()], outs=[y2_all[q].opt()]))
                cc.c += 0
        CC_Y2 = cc.c

        barrier(extra=(s_y2[0], s_y2[1], s_y2[2], s_y2[3]))
        stD.close()

        stE = ExitStack()
        rhs2 = sb("rhs2", [128, 32 * TOK], BF16, stE)
        wst_e = [sb(f"wste{i}", [128, 512], F32, stE) for i in range(4)]
        wbf_e = [sb(f"wbfe{i}", [128, 512], BF16, stE) for i in range(4)]
        u0t = [sb(f"u0t{i}", [128, TOK], F32, stE) for i in range(2)]
        v_sb = sb("v_sb", [128, NFC * TOK], F32)
        ps_o = [psum(f"ps_o{i}", [128, TOK], stE) for i in range(4)]

        s_r2 = sem()
        s_we = [sem() for _ in range(4)]
        s_u0t = [sem() for _ in range(2)]

        e0_tn = tn.c
        e0_cv = cv.c
        e0_cs = cs.c

        @blk.sync
        def _(sync):
            for arm in nc.switch_core_id(engines=[sync], n=128):
                rk = arm.logical % W
                sync.wait_ge(s_ag[rk // 2].h, 1)
                co = (rk % 2) * TOK
                for k in range(32):
                    sync.dma_start(
                        out=rhs2[:, k * TOK:(k + 1) * TOK],
                        in_=y2_all[rk // 2][k * 128:(k + 1) * 128,
                                            co:co + TOK]).then_inc(s_r2.h, 16)
            s_r2.c += 16 * 32
            for mq in range(4):
                for k in range(32):
                    j = mq * 32 + k
                    if j >= 4:
                        sync.wait_ge(cv.h, e0_cv + (j - 4) + 1)
                    s_we[j % 4].inc(sync.dma_start(
                        out=wst_e[j % 4][:],
                        in_=outpT[k * 128:(k + 1) * 128,
                                  mq * 512:(mq + 1) * 512]), 16)

        @blk.vector
        def _(vector):
            for j in range(128):
                vector.wait_ge(s_we[j % 4].h, 16 * (j // 4 + 1))
                if j >= 4:
                    vector.wait_ge(tn.h, e0_tn + 4 * (j - 3))
                vector.tensor_copy(out=wbf_e[j % 4][:],
                                   in_=wst_e[j % 4][:]).then_inc(cv.h, 1)
        cv.c += 128

        @blk.tensor
        def _(tensor):
            tensor.wait_ge(s_r2.h, s_r2.c)
            for mq in range(4):
                if mq >= 1:
                    tensor.wait_ge(cs.h, e0_cs + 4 * mq)
                for k in range(32):
                    j = mq * 32 + k
                    tensor.wait_ge(cv.h, e0_cv + j + 1)
                    ins = None
                    for m in range(4):
                        ins = mm512(tensor, ps_o[m],
                                    wbf_e[j % 4][:, m * 128:(m + 1) * 128],
                                    rhs2[:, k * TOK:(k + 1) * TOK],
                                    start=(k == 0), stop=(k == 31))
                    tn.inc(ins, 4)

        e1_cv = cv.c

        @blk.sync
        def _(sync):
            for fc in range(NFC):
                if fc >= 2:
                    sync.wait_ge(cv.h, e1_cv + (fc - 2) + 1)
                s_u0t[fc % 2].inc(sync.dma_start(
                    out=u0t[fc % 2][:],
                    in_=u0_sp[fc * 128:(fc + 1) * 128, :]), 16)

        @blk.scalar
        def _(scalar):
            for mq in range(4):
                for m in range(4):
                    fc = mq * 4 + m
                    scalar.wait_ge(tn.h, e0_tn + 128 * (mq + 1))
                    cs.inc(scalar.copy(out=v_sb[:, fc * TOK:(fc + 1) * TOK],
                                       in_=ps_o[m][:]))

        @blk.vector
        def _(vector):
            for fc in range(NFC):
                vector.wait_ge(cs.h, e0_cs + fc + 1)
                vector.wait_ge(s_u0t[fc % 2].h, 16 * (fc // 2 + 1))
                vector.tensor_add(out=v_sb[:, fc * TOK:(fc + 1) * TOK],
                                  in0=v_sb[:, fc * TOK:(fc + 1) * TOK],
                                  in1=u0t[fc % 2][:]).then_inc(cv.h, 1)
        cv.c += NFC

        barrier()
        stE.close()

        stG = ExitStack()
        ffh_sb = sb("ffh_sb", [128, 32 * TOK], BF16, stG)
        a2b_sb = sb("a2b_sb", [128, NFC * TOK], BF16, stG)
        stF = ExitStack()
        pstF = psum("pstF", [128, TOK], stF)
        psrF = psum("psrF", [128, TOK], stF)
        psqF = psum("psqF", [128, TOK], stF)
        scrF = (sb("sq0F", [128, TOK], F32, stF), sb("sq1F", [128, TOK], F32, stF),
                sb("mF", [1, TOK], F32, stF), sb("rF", [1, TOK], F32, stF),
                sb("qF", [1, TOK], F32, stF), sb("t1F", [128, TOK], F32, stF))
        emit_ln(lambda fc: v_sb[:, fc * TOK:(fc + 1) * TOK],
                lambda fc: a2b_sb[:, fc * TOK:(fc + 1) * TOK],
                ln2w_s, ln2b_s, (pstF, psrF, psqF), scrF,
                pre=((s_par, s_par.c),))
        barrier()
        stF.close()

        stG1 = ExitStack()
        wstg_p = [sb(f"wsp{i}", [128, NFC * 128], F32, stG1) for i in range(2)]
        wbf_a = [sb(f"wba{i}", [128, NFC * 128], BF16, stG1) for i in range(2)]
        wbf_g = [sb(f"wbg{i}", [128, NFC * 128], BF16, stG1) for i in range(2)]
        sg_sb = sb("sg_sb", [128, TOK], F32, stG1)
        ps_a = psum("ps_a", [128, TOK], stG1)
        ps_g = psum("ps_g", [128, TOK], stG1)

        s_wa = [sem() for _ in range(2)]
        s_wg = [sem() for _ in range(2)]
        g0_tn = tn.c
        g0_cs = cs.c
        g0_cv = cv.c

        @blk.sync
        def _(sync):
            for hc in range(32):
                if hc >= 1:
                    sync.wait_ge(cv.h, g0_cv + 3 * (hc - 1) + 1)
                s_wa[0].inc(sync.dma_start(
                    out=wstg_p[0][:].rearrange("p (k m) -> p k m", m=128),
                    in_=ffpT[:, hc * 128:(hc + 1) * 128].rearrange(
                        "(k p) m -> p k m", p=128)), 16)
                if hc >= 1:
                    sync.wait_ge(cv.h, g0_cv + 3 * (hc - 1) + 2)
                s_wg[0].inc(sync.dma_start(
                    out=wstg_p[1][:].rearrange("p (k m) -> p k m", m=128),
                    in_=ffpT[:, HH + hc * 128:HH + (hc + 1) * 128].rearrange(
                        "(k p) m -> p k m", p=128)), 16)

        @blk.vector
        def _(vector):
            for hc in range(32):
                vector.wait_ge(s_wa[0].h, 16 * (hc + 1))
                if hc >= 2:
                    vector.wait_ge(tn.h, g0_tn + 2 * (hc - 1))
                vector.tensor_copy(out=wbf_a[hc % 2][:],
                                   in_=wstg_p[0][:]).then_inc(cv.h, 1)
                vector.wait_ge(s_wg[0].h, 16 * (hc + 1))
                vector.wait_ge(cv.h, g0_cv + 3 * hc + 1)
                vector.tensor_copy(out=wbf_g[hc % 2][:],
                                   in_=wstg_p[1][:]).then_inc(cv.h, 1)
                vector.wait_ge(cs.h, g0_cs + hc + 1)
                vector.wait_ge(cv.h, g0_cv + 3 * hc + 2)
                vector.tensor_mul(out=ffh_sb[:, hc * TOK:(hc + 1) * TOK],
                                  in0=ps_a[:], in1=sg_sb[:]).then_inc(cv.h, 1)
        cv.c += 3 * 32

        @blk.tensor
        def _(tensor):
            for hc in range(32):
                tensor.wait_ge(cv.h, g0_cv + 3 * hc + 2)
                if hc >= 1:
                    tensor.wait_ge(cv.h, g0_cv + 3 * (hc - 1) + 3)
                for k in range(NFC):
                    mm512(tensor, ps_a, wbf_a[hc % 2][:, k * 128:(k + 1) * 128],
                          a2b_sb[:, k * TOK:(k + 1) * TOK],
                          start=(k == 0), stop=(k == NFC - 1))
                ins = None
                for k in range(NFC):
                    ins = mm512(tensor, ps_g,
                                wbf_g[hc % 2][:, k * 128:(k + 1) * 128],
                                a2b_sb[:, k * TOK:(k + 1) * TOK],
                                start=(k == 0), stop=(k == NFC - 1))
                tn.inc(ins, 2)

        @blk.scalar
        def _(scalar):
            for hc in range(32):
                scalar.wait_ge(tn.h, g0_tn + 2 * hc + 2)
                if hc >= 1:
                    scalar.wait_ge(cv.h, g0_cv + 3 * (hc - 1) + 3)
                cs.inc(scalar.activation(out=sg_sb[:], in_=ps_g[:],
                                         func=AF.Silu))

        barrier()
        stG1.close()

        stG2 = ExitStack()
        ffo_t = [sb(f"ffot{i}", [128, TOK], F32, stG2) for i in range(2)]
        xr_t = [sb(f"xrt{i}", [128, TOK], F32, stG2) for i in range(2)]
        wst_o = [sb(f"wsto{i}", [128, 512], F32, stG2) for i in range(4)]
        wbf_o = [sb(f"wbfo{i}", [128, 512], BF16, stG2) for i in range(4)]
        ps_f = [psum(f"ps_f{i}", [128, TOK], stG2) for i in range(4)]

        s_wo = [sem() for _ in range(4)]
        s_xr = [sem() for _ in range(2)]
        po = sem()
        g1_tn = tn.c
        g1_cv = cv.c
        g1_cs = cs.c

        @blk.sync
        def _(sync):
            for mq in range(4):
                for k in range(32):
                    j = mq * 32 + k
                    if j >= 4:
                        sync.wait_ge(cv.h, g1_cv + (j - 4) + 1)
                    s_wo[j % 4].inc(sync.dma_start(
                        out=wst_o[j % 4][:],
                        in_=ffoT[k * 128:(k + 1) * 128,
                                 mq * 512:(mq + 1) * 512]), 16)

        @blk.vector
        def _(vector):
            for j in range(128):
                vector.wait_ge(s_wo[j % 4].h, 16 * (j // 4 + 1))
                if j >= 4:
                    vector.wait_ge(tn.h, g1_tn + 4 * (j - 3))
                vector.tensor_copy(out=wbf_o[j % 4][:],
                                   in_=wst_o[j % 4][:]).then_inc(cv.h, 1)
        cv.c += 128

        @blk.tensor
        def _(tensor):
            for mq in range(4):
                if mq >= 1:
                    tensor.wait_ge(cs.h, g1_cs + 4 * mq)
                for k in range(32):
                    j = mq * 32 + k
                    tensor.wait_ge(cv.h, g1_cv + j + 1)
                    ins = None
                    for m in range(4):
                        ins = mm512(tensor, ps_f[m],
                                    wbf_o[j % 4][:, m * 128:(m + 1) * 128],
                                    ffh_sb[:, k * TOK:(k + 1) * TOK],
                                    start=(k == 0), stop=(k == 31))
                    tn.inc(ins, 4)

        g2_cv = cv.c

        @blk.scalar
        def _(scalar):
            for mq in range(4):
                for m in range(4):
                    fc = mq * 4 + m
                    scalar.wait_ge(tn.h, g1_tn + 128 * (mq + 1))
                    if fc >= 2:
                        scalar.wait_ge(po.h, 2 * (fc - 2) + 1)
                    cs.inc(scalar.copy(out=ffo_t[fc % 2][:], in_=ps_f[m][:]))

        @blk.gpsimd
        def _(gpsimd):
            for fc in range(NFC):
                if fc >= 2:
                    gpsimd.wait_ge(po.h, 2 * (fc - 2) + 2)
                s_xr[fc % 2].inc(gpsimd.dma_start(
                    out=xr_t[fc % 2][:],
                    in_=xT[fc * 128:(fc + 1) * 128, :]), 16)
                gpsimd.wait_ge(cs.h, g1_cs + fc + 1)
                gpsimd.tensor_add(out=v_sb[:, fc * TOK:(fc + 1) * TOK],
                                  in0=v_sb[:, fc * TOK:(fc + 1) * TOK],
                                  in1=ffo_t[fc % 2][:]).then_inc(po.h, 1)
                gpsimd.wait_ge(po.h, 2 * fc + 1)
                gpsimd.wait_ge(s_xr[fc % 2].h, 16 * (fc // 2 + 1))
                gpsimd.tensor_add(out=v_sb[:, fc * TOK:(fc + 1) * TOK],
                                  in0=v_sb[:, fc * TOK:(fc + 1) * TOK],
                                  in1=xr_t[fc % 2][:]).then_inc(po.h, 1)
                gpsimd.wait_ge(po.h, 2 * fc + 2)
        po.c += 2 * NFC
        x1_sb = v_sb

        barrier()
        stG2.close()
        stG.close()

        stH = ExitStack()
        u1b_sb = sb("u1b_sb", [128, NFC * TOK], BF16)
        pstH = psum("pstH", [128, TOK], stH)
        psrH = psum("psrH", [128, TOK], stH)
        psqH = psum("psqH", [128, TOK], stH)
        scrH = (sb("sq0H", [128, TOK], F32, stH), sb("sq1H", [128, TOK], F32, stH),
                sb("mH", [1, TOK], F32, stH), sb("rH", [1, TOK], F32, stH),
                sb("qH", [1, TOK], F32, stH), sb("t1H", [128, TOK], F32, stH))
        emit_ln(lambda fc: x1_sb[:, fc * TOK:(fc + 1) * TOK],
                lambda fc: u1b_sb[:, fc * TOK:(fc + 1) * TOK],
                None, None, (pstH, psrH, psqH), scrH)
        barrier()
        stH.close()

        stI = ExitStack()
        wbf_i = [sb(f"wbfi{i}", [128, 512], BF16, stI) for i in range(4)]
        out_t = [sb(f"outt{i}", [128, TOK], F32, stI) for i in range(2)]
        ps_i = [psum(f"ps_i{i}", [128, TOK], stI) for i in range(4)]

        s_wi = [sem() for _ in range(4)]
        so_p = [sem() for _ in range(2)]
        i0_tn = tn.c
        i0_cs = cs.c
        i0_cv = cv.c

        @blk.sync
        def _(sync):
            sync.wait_ge(cc.h, CC_FS)
            for mq in range(4):
                for k in range(NFC):
                    j = mq * NFC + k
                    if j >= 4:
                        sync.wait_ge(tn.h, i0_tn + 4 * (j - 3))
                        sync.wait_ge(s_wi[j % 4].h, 64 * ((j - 4) // 4 + 1))
                    ins = None
                    for m in range(4):
                        e = mq * 4 + m
                        r = e // 2
                        col = (e % 2) * 128
                        ins = sync.dma_start(
                            out=wbf_i[j % 4][:, m * 128:(m + 1) * 128],
                            in_=fs_all[r * D + k * 128:r * D + (k + 1) * 128,
                                       col:col + 128])
                        s_wi[j % 4].inc(ins, 16)

        @blk.tensor
        def _(tensor):
            for mq in range(4):
                if mq >= 1:
                    tensor.wait_ge(cs.h, i0_cs + 4 * mq)
                for k in range(NFC):
                    j = mq * NFC + k
                    tensor.wait_ge(s_wi[j % 4].h, 64 * (j // 4 + 1))
                    ins = None
                    for m in range(4):
                        ins = mm512(tensor, ps_i[m],
                                    wbf_i[j % 4][:, m * 128:(m + 1) * 128],
                                    u1b_sb[:, k * TOK:(k + 1) * TOK],
                                    start=(k == 0), stop=(k == NFC - 1))
                    tn.inc(ins, 4)

        @blk.scalar
        def _(scalar):
            scalar.wait_ge(s_par.h, s_par.c)
            for mq in range(4):
                for m in range(4):
                    fc = mq * 4 + m
                    scalar.wait_ge(tn.h, i0_tn + 64 * (mq + 1))
                    if fc >= 2:
                        scalar.wait_ge(so_p[fc % 2].h, 16 * (fc // 2))
                    cs.inc(scalar.activation(
                        out=out_t[fc % 2][:], in_=ps_i[m][:], func=AF.Identity,
                        scale=RSQRT8, bias=fbias_s[:, fc:fc + 1]))

        @blk.vector
        def _(vector):
            for fc in range(NFC):
                vector.wait_ge(cs.h, i0_cs + fc + 1)
                vector.tensor_add(out=out_t[fc % 2][:],
                                  in0=out_t[fc % 2][:],
                                  in1=x1_sb[:, fc * TOK:(fc + 1) * TOK]
                                  ).then_inc(cv.h, 1)
        cv.c += NFC

        @blk.gpsimd
        def _(gpsimd):
            for fc in range(NFC):
                gpsimd.wait_ge(cv.h, i0_cv + fc + 1)
                so_p[fc % 2].inc(gpsimd.dma_start(
                    out=out_fm[fc * 128:(fc + 1) * 128, :],
                    in_=out_t[fc % 2][:]), 16)
            gpsimd.wait_ge(so_p[0].h, so_p[0].c)
            gpsimd.wait_ge(so_p[1].h, so_p[1].c)

        barrier()
        stI.close()

    st.close()
    return nc


_NC_CACHE = {}


def kernel(**inputs):
    f32 = np.float32

    def cac(a):
        return np.ascontiguousarray(a, dtype=f32)

    x = np.asarray(inputs["L0_attn_x"])[0]
    xT = cac(x.T)
    w_in = np.asarray(inputs["L0_attn_in_proj_w"])
    wInXT = cac(w_in[:DI].T)
    wInZT = cac(w_in[DI:].T)
    conv_w = np.asarray(inputs["L0_attn_conv_w"])[:, 0, :]
    conv_b = np.asarray(inputs["L0_attn_conv_b"])
    xpT_f = cac(np.asarray(inputs["L0_attn_x_proj_w"]).T)
    dtpT_f = cac(np.asarray(inputs["L0_attn_dt_proj_w"]).T)
    dt_b = np.asarray(inputs["L0_attn_dt_proj_b"])
    A_log = np.asarray(inputs["L0_attn_A_log"])
    Dskip = np.asarray(inputs["L0_attn_D_skip"])
    outpT = cac(np.asarray(inputs["L0_attn_out_proj_w"]).T)
    ffpT = cac(np.asarray(inputs["L0_attn_ff_proj_w"]).T)
    ffoT = cac(np.asarray(inputs["L0_attn_ff_out_w"]).T)
    factors = np.asarray(inputs["L0_ffn_factors"])

    def col16(vec):
        return cac(np.asarray(vec).reshape(NFC, 128).T)

    def colg(vec):
        return cac(np.asarray(vec).reshape(NG, 128).T)

    ln1w = col16(inputs["L0_attn_ln1_w"]); ln1b = col16(inputs["L0_attn_ln1_b"])
    ln2w = col16(inputs["L0_attn_ln2_w"]); ln2b = col16(inputs["L0_attn_ln2_b"])
    fbias_t = col16(inputs["L0_ffn_bias"])

    in_maps = []
    for c in range(W):
        dsl = slice(c * DIC, (c + 1) * DIC)
        in_maps.append({
            "xT": cac(xT[:, c * TOK:(c + 1) * TOK]),
            "ln1w": ln1w, "ln1b": ln1b, "ln2w": ln2w, "ln2b": ln2b,
            "fbias": fbias_t,
            "wInXT": cac(wInXT[:, dsl]),
            "wInZT": cac(wInZT[:, dsl]),
            "convw": cac(conv_w[dsl].reshape(NG, 128, 4).transpose(1, 0, 2)
                         .reshape(128, 4 * NG)),
            "convb": colg(conv_b[dsl]),
            "dskip": colg(Dskip[dsl]),
            "dtb": colg(dt_b[dsl]),
            "alog": cac(A_log[dsl].reshape(NG, 128, DS).transpose(1, 0, 2)
                        .reshape(128, DS * NG)),
            "xpT": cac(xpT_f[dsl].reshape(NG, 128, 160).transpose(1, 0, 2)
                       .reshape(128, 160 * NG)),
            "dtpT": cac(dtpT_f[:, dsl]),
            "outpT": outpT,
            "ffpT": ffpT,
            "ffoT": ffoT,
            "fct": cac(factors[:, :, c * (D // W):(c + 1) * (D // W)]
                       .reshape(8 * D, D // W)),
            "consts": np.tile(np.array([[EPS, 1.0, -1.0, RSQRT8]], f32),
                              (128, 1)),
        })

    if "nc" not in _NC_CACHE:
        _NC_CACHE["nc"] = build_nc()
    nc = _NC_CACHE["nc"]
    res = run_bass_kernel_spmd(nc, in_maps, core_ids=list(range(W)))
    out = np.concatenate(
        [np.asarray(res.results[c]["out_fm"]).T for c in range(W)], axis=0)
    return np.ascontiguousarray(out[None], dtype=np.float32)


# version 16
# speedup vs baseline: 1.0145x; 1.0145x over previous
import numpy as np
from contextlib import ExitStack

import concourse.bass as bass
import concourse.mybir as mybir
from concourse.bass_utils import run_bass_kernel_spmd

F32 = mybir.dt.float32
BF16 = mybir.dt.bfloat16
AF = mybir.ActivationFunctionType
ALU = mybir.AluOpType
AX = mybir.AxisListType

W = 8
T = 8192
TOK = T // W
D = 2048
NFC = D // 128
DI = 4096
DIC = DI // W
NG = DIC // 128
DS = 16
DTR = 128
HH = 4096
TC = 256
NCH = T // TC
NU = NCH * NG
RG = [list(range(W))]
EPS = 1e-5
RSQRT8 = float(1.0 / np.sqrt(8.0))


class Sem:

    def __init__(self, handle):
        self.h = handle
        self.c = 0

    def inc(self, instr, k=1):
        instr.then_inc(self.h, k)
        self.c += k
        return self.c


def mm512(tensor, out, lhsT, rhs, start, stop, n=TOK):
    ins = None
    for c0 in range(0, n, 512):
        ins = tensor.matmul(out=out[:, c0:c0 + 512], lhsT=lhsT,
                            rhs=rhs[:, c0:c0 + 512], start=start, stop=stop)
    return ins


def build_nc():
    nc = bass.Bass("TRN2", target_bir_lowering=False)

    def inp(name, shape):
        return nc.dram_tensor(name, list(shape), F32, kind="ExternalInput").ap()

    xT = inp("xT", [D, TOK])
    ln1w = inp("ln1w", [128, NFC]); ln1b = inp("ln1b", [128, NFC])
    ln2w = inp("ln2w", [128, NFC]); ln2b = inp("ln2b", [128, NFC])
    fbias = inp("fbias", [128, NFC])
    wInXT = inp("wInXT", [D, DIC])
    wInZT = inp("wInZT", [D, DIC])
    convw = inp("convw", [128, 4 * NG])
    convb = inp("convb", [128, NG])
    dskip = inp("dskip", [128, NG])
    dtb = inp("dtb", [128, NG])
    alog = inp("alog", [128, DS * NG])
    xpT = inp("xpT", [128, 160 * NG])
    dtpT = inp("dtpT", [DTR, DIC])
    outpT = inp("outpT", [DI, D])
    ffpT = inp("ffpT", [D, 2 * HH])
    ffoT = inp("ffoT", [HH, D])
    fct = inp("fct", [8 * D, D // W])
    consts = inp("consts", [128, 4])

    out_fm = nc.dram_tensor("out_fm", [D, TOK], F32, kind="ExternalOutput").ap()

    def idram(name, shape, dt, shared=False):
        kw = dict(addr_space="Shared") if shared else {}
        return nc.dram_tensor(name, list(shape), dt, **kw).ap()

    a1b_own = idram("a1b_own", [D, TOK], BF16)
    a1b_all = idram("a1b_all", [W * D, TOK], BF16, shared=True)
    xdbl_part = idram("xdbl_part", [160, T], F32)
    xdbl = idram("xdbl", [160, T], F32, shared=True)
    xc_sp = idram("xc_sp", [DIC, T], F32)
    z_sp = idram("z_sp", [DIC, T], F32)
    u0_sp = idram("u0_sp", [D, TOK], F32)
    y2_own = [idram(f"y2_own{q}", [DIC, T // 4], BF16) for q in range(4)]
    y2_all = [idram(f"y2_all{q}", [DI, T // 4], BF16, shared=True)
              for q in range(4)]
    y2_loc = idram("y2_loc", [DI, TOK], BF16)
    fs_own = idram("fs_own", [D, D // W], BF16)
    fs_all = idram("fs_all", [W * D, D // W], BF16, shared=True)

    st = ExitStack()

    def sb(name, shape, dt, stk=None):
        side = "left" if stk is None else "right"
        return (stk or st).enter_context(
            nc.sbuf_tensor(name, list(shape), dt, side=side))

    def psum(name, shape, stk=None):
        return (stk or st).enter_context(nc.psum_tensor(name, list(shape), F32))

    _sem_i = [0]

    def sem():
        _sem_i[0] += 1
        return Sem(st.enter_context(nc.semaphore(f"s{_sem_i[0]}")))

    ones = sb("ones", [128, 128], F32)
    ln1w_s = sb("ln1w_s", [128, NFC], F32); ln1b_s = sb("ln1b_s", [128, NFC], F32)
    ln2w_s = sb("ln2w_s", [128, NFC], F32); ln2b_s = sb("ln2b_s", [128, NFC], F32)
    fbias_s = sb("fbias_s", [128, NFC], F32)
    convw_s = sb("convw_s", [128, 4 * NG], F32)
    convb_s = sb("convb_s", [128, NG], F32)
    dskip_s = sb("dskip_s", [128, NG], F32)
    dtb_s = sb("dtb_s", [128, NG], F32)
    a_s = sb("a_s", [128, DS * NG], F32)
    xp_bf = sb("xp_bf", [128, 160 * NG], BF16)
    dtp_bf = sb("dtp_bf", [DTR, DIC], BF16)
    consts_s = sb("consts_s", [128, 4], F32)
    for i, val in enumerate((EPS, 1.0, -1.0, RSQRT8)):
        nc.const_aps.register(F32, float(val), consts_s[:, i:i + 1])

    ph = sem()
    cc = sem()
    tn = sem()
    cv = sem()
    cs = sem()
    so = sem()
    ld = sem()

    ENG = ("sync", "vector", "scalar", "tensor", "gpsimd")
    bh = {}

    def barrier(extra=()):
        blk = bh["blk"]
        extra = [(s_, s_.c) for s_ in extra]
        target = ph.c + len(ENG)
        ph.c = target
        for e in ENG:
            def body(eng):
                eng.drain()
                eng.sem_inc(ph.h, 1)
                eng.wait_ge(ph.h, target)
                for s_, v_ in extra:
                    eng.wait_ge(s_.h, v_)
            getattr(blk, e)(body)

    def emit_ln(src_fn, dst_fn, w_s, b_s, lnp, scr, pre=()):
        blk = bh["blk"]
        pst, psr, psq = lnp
        sq0, sq1, m_row, r_row, q_row, t1 = scr
        sqs = (sq0, sq1)
        cs0, tn0 = cs.c, tn.c

        @blk.scalar
        def _(scalar):
            for s_, v_ in pre:
                scalar.wait_ge(s_.h, v_)
            for fc in range(NFC):
                if fc >= 2:
                    scalar.wait_ge(tn.h, tn0 + fc - 1)
                cs.inc(scalar.activation(out=sqs[fc % 2][:], in_=src_fn(fc),
                                         func=AF.Square))

        @blk.tensor
        def _(tensor):
            for s_, v_ in pre:
                tensor.wait_ge(s_.h, v_)
            for fc in range(NFC):
                for hf in range(2):
                    sl = slice(hf * 512, (hf + 1) * 512)
                    tensor.matmul(out=pst[0:1, sl], lhsT=ones[:, 0:1],
                                  rhs=src_fn(fc)[:, sl],
                                  start=(fc == 0), stop=(fc == NFC - 1))
                tensor.wait_ge(cs.h, cs0 + fc + 1)
                ins = None
                for hf in range(2):
                    sl = slice(hf * 512, (hf + 1) * 512)
                    ins = tensor.matmul(out=pst[32:33, sl], lhsT=ones[:, 0:1],
                                        rhs=sqs[fc % 2][:, sl],
                                        start=(fc == 0), stop=(fc == NFC - 1))
                tn.inc(ins)

        cv0 = cv.c

        @blk.vector
        def _(vector):
            vector.wait_ge(tn.h, tn.c)
            vector.tensor_scalar(out=m_row[:], in0=pst[0:1, :], scalar1=1.0 / D,
                                 scalar2=None, op0=ALU.mult).then_inc(cv.h, 1)
            vector.wait_ge(cv.h, cv0 + 1)
            vector.tensor_scalar(out=r_row[:], in0=pst[32:33, :], scalar1=1.0 / D,
                                 scalar2=None, op0=ALU.mult).then_inc(cv.h, 1)
            vector.wait_ge(cv.h, cv0 + 2)
            vector.tensor_mul(out=q_row[:], in0=m_row[:],
                              in1=m_row[:]).then_inc(cv.h, 1)
            vector.wait_ge(cv.h, cv0 + 3)
            vector.tensor_sub(out=r_row[:], in0=r_row[:],
                              in1=q_row[:]).then_inc(cv.h, 1)
        cv.c += 4

        @blk.scalar
        def _(scalar):
            scalar.wait_ge(cv.h, cv.c)
            cs.inc(scalar.activation(out=q_row[:], in_=r_row[:], func=AF.Sqrt,
                                     bias=EPS))

        cv1 = cv.c

        @blk.vector
        def _(vector):
            vector.wait_ge(cs.h, cs.c)
            vector.reciprocal(out=r_row[:], in_=q_row[:]).then_inc(cv.h, 1)
            vector.wait_ge(cv.h, cv1 + 1)
            vector.tensor_mul(out=q_row[:], in0=m_row[:],
                              in1=r_row[:]).then_inc(cv.h, 1)
            vector.wait_ge(cv.h, cv1 + 2)
            vector.tensor_scalar(out=q_row[:], in0=q_row[:], scalar1=-1.0,
                                 scalar2=None, op0=ALU.mult).then_inc(cv.h, 1)
        cv.c += 3

        @blk.tensor
        def _(tensor):
            tensor.wait_ge(cv.h, cv.c)
            for half in range(2):
                sl = slice(half * 512, (half + 1) * 512)
                tensor.matmul(out=psr[:, sl], lhsT=ones[0:1, :], rhs=r_row[:, sl],
                              start=True, stop=True)
                tn.inc(tensor.matmul(out=psq[:, sl], lhsT=ones[0:1, :],
                                     rhs=q_row[:, sl], start=True, stop=True))

        cv2 = cv.c

        @blk.vector
        def _(vector):
            vector.wait_ge(tn.h, tn.c)
            for fc in range(NFC):
                vector.tensor_mul(out=t1[:], in0=src_fn(fc),
                                  in1=psr[:]).then_inc(cv.h, 1)
                vector.wait_ge(cv.h, cv2 + 3 * fc + 1)
                if w_s is None:
                    vector.tensor_add(out=dst_fn(fc), in0=t1[:],
                                      in1=psq[:]).then_inc(cv.h, 2)
                    vector.wait_ge(cv.h, cv2 + 3 * fc + 3)
                else:
                    vector.tensor_add(out=t1[:], in0=t1[:],
                                      in1=psq[:]).then_inc(cv.h, 1)
                    vector.wait_ge(cv.h, cv2 + 3 * fc + 2)
                    vector.tensor_scalar(
                        out=dst_fn(fc), in0=t1[:],
                        scalar1=w_s[:, fc:fc + 1], scalar2=b_s[:, fc:fc + 1],
                        op0=ALU.mult, op1=ALU.add).then_inc(cv.h, 1)
                    vector.wait_ge(cv.h, cv2 + 3 * fc + 3)
        cv.c += 3 * NFC

    with nc.Block() as blk:
        bh["blk"] = blk

        stA = ExitStack()
        stAB = ExitStack()
        stABC = ExitStack()
        wInX_bf = sb("wInX_bf", [128, NFC * 512], BF16, stABC)
        wInZ_bf = sb("wInZ_bf", [128, NFC * 512], BF16, stABC)
        x_sb = sb("x_sb", [128, NFC * TOK], F32, stAB)
        wstg = sb("wstg", [128, NFC * 512], F32, stA)
        fstg = sb("fstg", [128, NFC * 256], F32, stA)
        facc = sb("facc", [128, NFC * 256], F32, stA)
        fbf = sb("fbf", [128, NFC * 256], BF16, stA)
        xp_st = sb("xp_st", [128, 160 * NG], F32, stA)
        dtp_st = sb("dtp_st", [DTR, DIC], F32, stA)
        ascr = sb("ascr", [128, DS * NG], F32, stA)

        s_par = sem()
        s_wx = sem()
        s_wz = sem()
        s_fc = [sem() for _ in range(2)]
        s_fs = sem()

        @blk.sync
        def _(sync):
            s_par.inc(sync.dma_start(
                out=x_sb[:].rearrange("p (f t) -> p f t", t=TOK),
                in_=xT.rearrange("(f p) t -> p f t", p=128)), 16)
            for t_, d_ in ((ln1w, ln1w_s), (ln1b, ln1b_s), (ln2w, ln2w_s),
                           (ln2b, ln2b_s), (fbias, fbias_s), (convw, convw_s),
                           (convb, convb_s), (dskip, dskip_s), (dtb, dtb_s),
                           (alog, a_s), (xpT, xp_st), (dtpT, dtp_st),
                           (consts, consts_s)):
                s_par.inc(sync.dma_start(out=d_[:], in_=t_), 16)
            s_wx.inc(sync.dma_start(
                out=wstg[:].rearrange("p (k m) -> p k m", m=512),
                in_=wInXT.rearrange("(k p) m -> p k m", p=128)), 16)

        cvA0 = cv.c

        @blk.vector
        def _(vector):
            vector.memset(ones[:], 1.0).then_inc(cv.h, 1)
            vector.wait_ge(s_par.h, s_par.c)
            vector.tensor_copy(out=xp_bf[:], in_=xp_st[:]).then_inc(cv.h, 1)
            vector.tensor_copy(out=dtp_bf[:], in_=dtp_st[:]).then_inc(cv.h, 1)
            vector.wait_ge(s_wx.h, s_wx.c)
            vector.tensor_copy(out=wInX_bf[:], in_=wstg[:]).then_inc(cv.h, 1)
        cv.c += 4

        @blk.sync
        def _(sync):
            sync.wait_ge(cv.h, cv.c)
            s_wz.inc(sync.dma_start(
                out=wstg[:].rearrange("p (k m) -> p k m", m=512),
                in_=wInZT.rearrange("(k p) m -> p k m", p=128)), 16)

        @blk.vector
        def _(vector):
            vector.wait_ge(s_wz.h, s_wz.c)
            vector.tensor_copy(out=wInZ_bf[:], in_=wstg[:]).then_inc(cv.h, 1)
        cv.c += 1

        @blk.scalar
        def _(scalar):
            scalar.wait_ge(s_par.h, s_par.c)
            cs.inc(scalar.activation(out=ascr[:], in_=a_s[:], func=AF.Exp))

        @blk.vector
        def _(vector):
            vector.wait_ge(cs.h, cs.c)
            vector.tensor_scalar(out=a_s[:], in0=ascr[:], scalar1=-1.0,
                                 scalar2=None, op0=ALU.mult).then_inc(cv.h, 1)
        cv.c += 1

        cvF0 = cv.c

        @blk.gpsimd
        def _(gpsimd):
            for r in range(8):
                if r >= 2:
                    gpsimd.wait_ge(cv.h, cvF0 + (r - 1))
                dst = facc if r == 0 else fstg
                s_fc[r % 2].inc(gpsimd.dma_start(
                    out=dst[:].rearrange("p (f e) -> p f e", e=256),
                    in_=fct[r * D:(r + 1) * D, :].rearrange(
                        "(f p) e -> p f e", p=128)), 16)

        fc_ms = {}
        cnt = [0, 0]
        for r in range(8):
            cnt[r % 2] += 16
            fc_ms[r] = (r % 2, cnt[r % 2])

        @blk.vector
        def _(vector):
            vector.wait_ge(s_fc[0].h, 16)
            for r in range(1, 8):
                p_, v_ = fc_ms[r]
                vector.wait_ge(s_fc[p_].h, v_)
                vector.wait_ge(cv.h, cvF0 + (r - 1))
                vector.tensor_add(out=facc[:], in0=facc[:],
                                  in1=fstg[:]).then_inc(cv.h, 1)
            vector.wait_ge(cv.h, cvF0 + 7)
            vector.tensor_copy(out=fbf[:], in_=facc[:]).then_inc(cv.h, 1)
        cv.c += 8

        @blk.sync
        def _(sync):
            sync.wait_ge(cv.h, cv.c)
            s_fs.inc(sync.dma_start(
                out=fs_own.rearrange("(f p) e -> p f e", p=128),
                in_=fbf[:].rearrange("p (f e) -> p f e", e=256)), 16)

        @blk.gpsimd
        def _(gpsimd):
            gpsimd.wait_ge(s_fs.h, s_fs.c)
            cc.inc(gpsimd.collective_compute(
                "AllGather", ALU.bypass, replica_groups=RG,
                ins=[fs_own.opt()], outs=[fs_all.opt()]))
        CC_FS = cc.c

        barrier(extra=(s_fs,))
        stA.close()

        stB = ExitStack()
        u0_sb = x_sb
        a1b_sb = sb("a1b_sb", [128, NFC * TOK], BF16, stB)
        pstB = psum("pstB", [128, TOK], stB)
        psrB = psum("psrB", [128, TOK], stB)
        psqB = psum("psqB", [128, TOK], stB)
        scrB = (sb("sq0", [128, TOK], F32, stB), sb("sq1", [128, TOK], F32, stB),
                sb("m_row", [1, TOK], F32, stB), sb("r_row", [1, TOK], F32, stB),
                sb("q_row", [1, TOK], F32, stB), sb("t1B", [128, TOK], F32, stB))

        emit_ln(lambda fc: x_sb[:, fc * TOK:(fc + 1) * TOK],
                lambda fc: u0_sb[:, fc * TOK:(fc + 1) * TOK],
                None, None, (pstB, psrB, psqB), scrB,
                pre=((s_par, s_par.c),))
        emit_ln(lambda fc: u0_sb[:, fc * TOK:(fc + 1) * TOK],
                lambda fc: a1b_sb[:, fc * TOK:(fc + 1) * TOK],
                ln1w_s, ln1b_s, (pstB, psrB, psqB), scrB,
                pre=((cv, cv.c), (s_par, s_par.c)))

        s_u0sp = sem()
        s_a1 = sem()

        @blk.sync
        def _(sync):
            sync.wait_ge(cv.h, cv.c)
            s_u0sp.inc(sync.dma_start(
                out=u0_sp.rearrange("(f p) t -> p f t", p=128),
                in_=u0_sb[:].rearrange("p (f t) -> p f t", t=TOK)), 16)
            s_a1.inc(sync.dma_start(
                out=a1b_own.rearrange("(f p) t -> p f t", p=128),
                in_=a1b_sb[:].rearrange("p (f t) -> p f t", t=TOK)), 16)

        @blk.gpsimd
        def _(gpsimd):
            gpsimd.wait_ge(s_a1.h, s_a1.c)
            cc.inc(gpsimd.collective_compute(
                "AllGather", ALU.bypass, replica_groups=RG,
                ins=[a1b_own.opt()], outs=[a1b_all.opt()]))
        CC_A1 = cc.c

        barrier(extra=(s_u0sp, s_a1))
        stB.close()
        stAB.close()

        stC = ExitStack()
        wInX_bf2 = wInX_bf
        wInZ_bf2 = wInZ_bf
        rhs_t = [sb(f"rhs{i}", [128, NFC * TOK], BF16, stC) for i in range(2)]
        ps_in = [psum(f"ps_in{i}", [128, TOK], stC) for i in range(2)]
        ps_xdt = psum("ps_xdt", [128, TOK], stC)
        ps_xbc = psum("ps_xbc", [32, TOK], stC)
        xm_t = [sb(f"xm{g}", [128, TOK + 3], F32, stC) for g in range(NG)]
        xc_t = [sb(f"xct{g}", [128, TOK], F32, stC) for g in range(NG)]
        xcb_t = [sb(f"xcb{g}", [128, TOK], BF16, stC) for g in range(NG)]
        z_t = [sb(f"zt{g}", [128, TOK], F32, stC) for g in range(NG)]
        xdt_ev = sb("xdt_ev", [128, TOK], F32, stC)
        xbc_ev = sb("xbc_ev", [32, TOK], F32, stC)

        s_rhs = [sem() for _ in range(2)]
        s_xcz = sem()
        s_xdb = sem()

        tc0_tn = tn.c
        tc0_cs = cs.c
        tc0_cv = cv.c

        def ev_ms(u):
            return tc0_cs + (u // 8) * 12 + (u % 8) + 1

        @blk.vector
        def _(vector):
            for g in range(NG):
                vector.memset(xm_t[g][:, 0:3], 0.0).then_inc(cv.h, 1)
        cv.c += NG
        tc0_cv = cv.c

        @blk.sync
        def _(sync):
            sync.wait_ge(cc.h, CC_A1)
            for tc in range(W):
                base = tc * D
                if tc >= 2:
                    sync.wait_ge(tn.h, tc0_tn + (tc - 2) * 10 + 8)
                for k in range(NFC):
                    s_rhs[tc % 2].inc(sync.dma_start(
                        out=rhs_t[tc % 2][:, k * TOK:(k + 1) * TOK],
                        in_=a1b_all[base + k * 128: base + (k + 1) * 128, :]), 16)

        @blk.tensor
        def _(tensor):
            for tc in range(W):
                tensor.wait_ge(s_rhs[tc % 2].h, 16 * NFC * (tc // 2 + 1))
                for mg in range(8):
                    u = tc * 8 + mg
                    if u >= 2:
                        tensor.wait_ge(cs.h, ev_ms(u - 2))
                    wsel = wInX_bf2 if mg < 4 else wInZ_bf2
                    msub = mg % 4
                    ins = None
                    for k in range(NFC):
                        ins = mm512(tensor, ps_in[u % 2],
                                    wsel[:, k * 512 + msub * 128:
                                         k * 512 + (msub + 1) * 128],
                                    rhs_t[tc % 2][:, k * TOK:(k + 1) * TOK],
                                    start=(k == 0), stop=(k == NFC - 1))
                    tn.inc(ins)
                tensor.wait_ge(cv.h, tc0_cv + tc * 26 + 24)
                if tc >= 1:
                    tensor.wait_ge(cv.h, tc0_cv + (tc - 1) * 26 + 26)
                ins = None
                for g in range(NG):
                    mm512(tensor, ps_xdt, xp_bf[:, g * 160:g * 160 + 128],
                          xcb_t[g][:], start=(g == 0), stop=(g == NG - 1))
                    ins = mm512(tensor, ps_xbc,
                                xp_bf[:, g * 160 + 128:g * 160 + 160],
                                xcb_t[g][:], start=(g == 0), stop=(g == NG - 1))
                tn.inc(ins, 2)

        @blk.scalar
        def _(scalar):
            for tc in range(W):
                for mg in range(8):
                    scalar.wait_ge(tn.h, tc0_tn + tc * 10 + mg + 1)
                    if mg < 4:
                        if tc >= 1:
                            scalar.wait_ge(cv.h,
                                           tc0_cv + (tc - 1) * 26 + mg * 5 + 5)
                        cs.inc(scalar.copy(out=xm_t[mg][:, 3:3 + TOK],
                                           in_=ps_in[(tc * 8 + mg) % 2][:]))
                    else:
                        if tc >= 1:
                            scalar.wait_ge(s_xcz.h, 128 * tc)
                        cs.inc(scalar.activation(
                            out=z_t[mg - 4][:], in_=ps_in[(tc * 8 + mg) % 2][:],
                            func=AF.Silu))
                for g in range(NG):
                    scalar.wait_ge(cv.h, tc0_cv + tc * 26 + g * 5 + 4)
                    cs.inc(scalar.activation(out=xc_t[g][:], in_=xc_t[g][:],
                                             func=AF.Silu,
                                             bias=convb_s[:, g:g + 1]))

        @blk.vector
        def _(vector):
            vector.wait_ge(s_par.h, s_par.c)
            for tc in range(W):
                cvb = tc0_cv + tc * 26
                for g in range(NG):
                    vector.wait_ge(cs.h, tc0_cs + tc * 12 + g + 1)
                    if tc >= 1:
                        vector.wait_ge(s_xcz.h, 128 * tc)
                        vector.wait_ge(cv.h, tc0_cv + (tc - 1) * 26 + g * 5 + 5)
                    else:
                        vector.wait_ge(cv.h, tc0_cv)
                    vector.tensor_scalar(
                        out=xc_t[g][:], in0=xm_t[g][:, 0:TOK],
                        scalar1=convw_s[:, g * 4:g * 4 + 1], scalar2=None,
                        op0=ALU.mult).then_inc(cv.h, 1)
                    for k in range(1, 4):
                        vector.wait_ge(cv.h, cvb + g * 5 + k)
                        vector.scalar_tensor_tensor(
                            out=xc_t[g][:], in0=xm_t[g][:, k:k + TOK],
                            scalar=convw_s[:, g * 4 + k:g * 4 + k + 1],
                            in1=xc_t[g][:], op0=ALU.mult,
                            op1=ALU.add).then_inc(cv.h, 1)
                    vector.wait_ge(cv.h, cvb + g * 5 + 4)
                    vector.tensor_copy(out=xm_t[g][:, 0:3],
                                       in_=xm_t[g][:, TOK:TOK + 3]
                                       ).then_inc(cv.h, 1)
                for g in range(NG):
                    vector.wait_ge(cs.h, tc0_cs + tc * 12 + 8 + g + 1)
                    vector.tensor_copy(out=xcb_t[g][:],
                                       in_=xc_t[g][:]).then_inc(cv.h, 1)
                vector.wait_ge(tn.h, tc0_tn + tc * 10 + 10)
                if tc >= 1:
                    vector.wait_ge(s_xdb.h, 32 * tc)
                vector.tensor_copy(out=xdt_ev[:], in_=ps_xdt[:]).then_inc(cv.h, 1)
                vector.tensor_copy(out=xbc_ev[:], in_=ps_xbc[:]).then_inc(cv.h, 1)
        cv.c += 26 * W

        @blk.gpsimd
        def _(gpsimd):
            for tc in range(W):
                for g in range(NG):
                    gpsimd.wait_ge(cs.h, tc0_cs + tc * 12 + 8 + g + 1)
                    s_xcz.inc(gpsimd.dma_start(
                        out=xc_sp[g * 128:(g + 1) * 128,
                                  tc * TOK:(tc + 1) * TOK],
                        in_=xc_t[g][:]), 16)
                    gpsimd.wait_ge(cs.h, tc0_cs + tc * 12 + 4 + g + 1)
                    s_xcz.inc(gpsimd.dma_start(
                        out=z_sp[g * 128:(g + 1) * 128,
                                 tc * TOK:(tc + 1) * TOK],
                        in_=z_t[g][:]), 16)
                gpsimd.wait_ge(cv.h, tc0_cv + tc * 26 + 26)
                s_xdb.inc(gpsimd.dma_start(
                    out=xdbl_part[0:128, tc * TOK:(tc + 1) * TOK],
                    in_=xdt_ev[:]), 16)
                s_xdb.inc(gpsimd.dma_start(
                    out=xdbl_part[128:160, tc * TOK:(tc + 1) * TOK],
                    in_=xbc_ev[:]), 16)

        @blk.gpsimd
        def _(gpsimd):
            gpsimd.wait_ge(s_xdb.h, s_xdb.c)
            cc.inc(gpsimd.collective_compute(
                "AllReduce", ALU.add, replica_groups=RG,
                ins=[xdbl_part.opt()], outs=[xdbl.opt()]))
        CC_XD = cc.c

        barrier(extra=(s_xcz, s_xdb))
        stC.close()
        stABC.close()

        stD = ExitStack()
        dtraw_bf = sb("dtraw_bf", [DTR, T], BF16, stD)
        bt_t = [sb(f"bt{i}", [128, DS * TC], F32, stD) for i in range(2)]
        ct_t = [sb(f"ct{i}", [128, DS * TC], F32, stD) for i in range(2)]
        g_t = [sb(f"gt{i}", [128, DS * TC], F32, stD) for i in range(3)]
        u_sb = [sb(f"u_sb{i}", [128, DS * TC], F32, stD) for i in range(2)]
        h_sb = sb("h_sb", [128, DS * TC], F32, stD)
        hst = sb("hst", [128, DS * NG], F32, stD)
        tmp16 = sb("tmp16", [128, DS], F32, stD)
        dt_t = [sb(f"dt{i}", [128, TC], F32, stD) for i in range(3)]
        w2_t = [sb(f"w2_{i}", [128, 2 * TC], F32, stD) for i in range(3)]
        tabs = sb("tabs", [128, TC], F32, stD)
        texp = sb("texp", [128, TC], F32, stD)
        tln = [sb(f"tln{i}", [128, TC], F32, stD) for i in range(2)]
        trelu = [sb(f"trelu{i}", [128, TC], F32, stD) for i in range(2)]
        xcg_t = [sb(f"xcg{i}", [128, TC], F32, stD) for i in range(4)]
        zg_t = [sb(f"zg{i}", [128, TC], F32, stD) for i in range(4)]
        y_sb = sb("y_sb", [128, TC], F32, stD)
        y2b_t = [sb(f"y2b{i}", [128, TC], BF16, stD) for i in range(4)]
        ps_dt = [psum(f"ps_dt{i}", [128, TC], stD) for i in range(3)]

        s_dst = [sem() for _ in range(2)]
        s_xcg = [sem() for _ in range(4)]
        s_zg = [sem() for _ in range(4)]
        s_bc = [sem() for _ in range(2)]
        s_y2 = [sem() for _ in range(4)]

        pp = sem()
        d0_cv = cv.c
        d0_cs = cs.c
        d0_tn = tn.c
        VI = 17
        PI = 1
        SI = 5
        SPL = 10

        @blk.sync
        def _(sync):
            sync.wait_ge(cc.h, CC_XD)
            for j in range(4):
                if j >= 2:
                    sync.wait_ge(cv.h, d0_cv + (j - 2) + 1)
                s_dst[j % 2].inc(sync.dma_start(
                    out=bt_t[0][:, (j % 2) * 2048:(j % 2 + 1) * 2048],
                    in_=xdbl[0:128, j * 2048:(j + 1) * 2048]), 16)

        @blk.vector
        def _(vector):
            for j in range(4):
                vector.wait_ge(s_dst[j % 2].h, 16 * (j // 2 + 1))
                vector.tensor_copy(out=dtraw_bf[:, j * 2048:(j + 1) * 2048],
                                   in_=bt_t[0][:, (j % 2) * 2048:(j % 2 + 1) * 2048]
                                   ).then_inc(cv.h, 1)
            vector.wait_ge(cv.h, d0_cv + 4)
            vector.memset(hst[:], 0.0).then_inc(cv.h, 1)
        cv.c += 5

        d1_cv = cv.c

        HEAD = ("dtadd", "wm", "wc")
        TAIL = ("ub", "foldm", "folda", "memset", "scan", "hst", "rmult",
                "tree1", "tree2", "tree3", "comb", "yfold", "stt", "y2b")
        sched = []
        for u in range(NU):
            sched.extend((op, u) for op in HEAD)
            if u >= 2:
                sched.extend((op, u - 2) for op in TAIL)
        sched.extend((op, NU - 2) for op in TAIL)
        sched.extend((op, NU - 1) for op in TAIL)
        L = {item: d1_cv + i + 1 for i, item in enumerate(sched)}

        def P(op, u):
            return u + 1

        @blk.sync
        def _(sync):
            sync.wait_ge(cc.h, CC_XD)
            for ch in range(NCH):
                c0 = ch * TC
                if ch == 0:
                    sync.wait_ge(cv.h, d0_cv + 4)
                if ch >= 2:
                    lastu = NG * (ch - 2) + (NG - 1)
                    sync.wait_ge(cv.h, L[("rmult", lastu)])
                for s in range(DS):
                    s_bc[ch % 2].inc(sync.dma_start(
                        out=bt_t[ch % 2][:, s * TC:(s + 1) * TC],
                        in_=xdbl[128 + s:129 + s, c0:c0 + TC]
                        .partition_broadcast(128)), 16)
                    s_bc[ch % 2].inc(sync.dma_start(
                        out=ct_t[ch % 2][:, s * TC:(s + 1) * TC],
                        in_=xdbl[144 + s:145 + s, c0:c0 + TC]
                        .partition_broadcast(128)), 16)
                for g in range(NG):
                    u = ch * NG + g
                    if u >= 4:
                        sync.wait_ge(cv.h, L[("stt", u - 4)])
                    s_xcg[u % 4].inc(sync.dma_start(
                        out=xcg_t[u % 4][:],
                        in_=xc_sp[g * 128:(g + 1) * 128,
                                  ch * TC:(ch + 1) * TC]), 16)
                    if u >= 4:
                        sync.wait_ge(cv.h, L[("y2b", u - 4)])
                    s_zg[u % 4].inc(sync.dma_start(
                        out=zg_t[u % 4][:],
                        in_=z_sp[g * 128:(g + 1) * 128,
                                 ch * TC:(ch + 1) * TC]), 16)

        @blk.gpsimd
        def _(gpsimd):
            gpsimd.wait_ge(cc.h, CC_XD)
            for ch in range(NCH):
                for g in range(NG):
                    u = ch * NG + g
                    if ch >= 1:
                        up = (ch - 1) * NG + g
                        qq, qc = (ch - 1) // 8, (ch - 1) % 8
                        gpsimd.wait_ge(cv.h, L[("y2b", up)])
                        s_y2[g].inc(gpsimd.dma_start(
                            out=y2_own[qq][g * 128:(g + 1) * 128,
                                          qc * TC:(qc + 1) * TC],
                            in_=y2b_t[g][:]), 16)
        pp.c += 0

        @blk.gpsimd
        def _(gpsimd):
            for g in range(NG):
                up = (NCH - 1) * NG + g
                gpsimd.wait_ge(cv.h, L[("y2b", up)])
                s_y2[g].inc(gpsimd.dma_start(
                    out=y2_own[3][g * 128:(g + 1) * 128, 7 * TC:],
                    in_=y2b_t[g][:]), 16)

        @blk.tensor
        def _(tensor):
            tensor.wait_ge(cv.h, d0_cv + 4)
            for ch in range(NCH):
                for g in range(NG):
                    u = ch * NG + g
                    if u >= 3:
                        tensor.wait_ge(cs.h, d0_cs + SI * (u - 3) + 4)
                    tn.inc(tensor.matmul(out=ps_dt[u % 3][:],
                                         lhsT=dtp_bf[:, g * 128:(g + 1) * 128],
                                         rhs=dtraw_bf[:, ch * TC:(ch + 1) * TC],
                                         start=True, stop=True))

        @blk.scalar
        def _(scalar):
            scalar.wait_ge(s_par.h, s_par.c)
            for ch in range(NCH):
                for g in range(NG):
                    u = ch * NG + g
                    csb = d0_cs + SI * u
                    scalar.wait_ge(tn.h, d0_tn + u + 1)
                    scalar.activation(out=tabs[:], in_=ps_dt[u % 3][:],
                                      func=AF.Abs,
                                      bias=dtb_s[:, g:g + 1]).then_inc(cs.h, 1)
                    scalar.wait_ge(cs.h, csb + 1)
                    scalar.activation(out=texp[:], in_=tabs[:], func=AF.Exp,
                                      scale=-1.0).then_inc(cs.h, 1)
                    scalar.wait_ge(cs.h, csb + 2)
                    if u >= 2:
                        scalar.wait_ge(cv.h, L[("dtadd", u - 2)])
                    scalar.activation(out=tln[u % 2][:], in_=texp[:], func=AF.Ln,
                                      bias=1.0).then_inc(cs.h, 1)
                    scalar.activation(out=trelu[u % 2][:], in_=ps_dt[u % 3][:],
                                      func=AF.Relu,
                                      bias=dtb_s[:, g:g + 1]).then_inc(cs.h, 1)
                    scalar.wait_ge(cv.h, L[("dtadd", u)])
                    if u >= 3:
                        scalar.wait_ge(cv.h, L[("scan", u - 3)])
                    ins = None
                    for s in range(DS):
                        ins = scalar.activation(
                            out=g_t[u % 3][:, s * TC:(s + 1) * TC],
                            in_=dt_t[u % 3][:], func=AF.Exp,
                            scale=a_s[:, g * DS + s:g * DS + s + 1])
                    ins.then_inc(cs.h, 1)
        cs.c += SI * NU

        def emit_head(vector, u):
            csb = d0_cs + SI * u
            vector.wait_ge(cs.h, csb + 4)
            if u >= 3:
                vector.wait_ge(cs.h, d0_cs + SI * (u - 3) + 5)
                vector.wait_ge(cv.h, L[("wc", u - 3)])
            vector.tensor_add(out=dt_t[u % 3][:], in0=tln[u % 2][:],
                              in1=trelu[u % 2][:]).then_inc(cv.h, 1)
            vector.wait_ge(s_xcg[u % 4].h, 16 * (u // 4 + 1))
            vector.wait_ge(cv.h, L[("dtadd", u)])
            if u >= 3:
                vector.wait_ge(cv.h, L[("ub", u - 3)])
            vector.tensor_mul(out=w2_t[u % 3][:, 0:TC], in0=dt_t[u % 3][:],
                              in1=xcg_t[u % 4][:]).then_inc(cv.h, 1)
            vector.wait_ge(cv.h, L[("wm", u)])
            vector.tensor_copy(out=w2_t[u % 3][:, TC:2 * TC],
                               in_=w2_t[u % 3][:, 0:TC]).then_inc(cv.h, 1)

        def emit_tail(vector, u):
            ch, g = u // NG, u % NG
            csb = d0_cs + SI * u
            vector.wait_ge(s_bc[ch % 2].h, 512 * (ch // 2 + 1))
            vector.wait_ge(cv.h, L[("wc", u)])
            ins = None
            for s2 in range(0, DS, 2):
                ins = vector.tensor_mul(
                    out=u_sb[u % 2][:, s2 * TC:(s2 + 2) * TC],
                    in0=w2_t[u % 3][:],
                    in1=bt_t[ch % 2][:, s2 * TC:(s2 + 2) * TC])
            ins.then_inc(cv.h, 1)
            vector.wait_ge(cs.h, csb + 5)
            vector.wait_ge(cv.h, L[("ub", u)])
            if u >= 4:
                vector.wait_ge(cv.h, L[("hst", u - 4)])
            gst = g_t[u % 3][:, 0::TC]
            ust = u_sb[u % 2][:, 0::TC]
            vector.tensor_mul(out=tmp16[:], in0=gst,
                              in1=hst[:, g * DS:(g + 1) * DS]).then_inc(cv.h, 1)
            vector.wait_ge(cv.h, L[("foldm", u)])
            vector.tensor_add(out=ust, in0=ust, in1=tmp16[:]).then_inc(cv.h, 1)
            vector.wait_ge(cv.h, L[("folda", u)])
            vector.memset(gst, 0.0).then_inc(cv.h, 1)
            vector.wait_ge(cv.h, L[("memset", u)])
            vector.tensor_tensor_scan(
                out=h_sb[:], data0=g_t[u % 3][:], data1=u_sb[u % 2][:],
                initial=0.0, op0=ALU.mult, op1=ALU.add).then_inc(cv.h, 1)
            vector.wait_ge(cv.h, L[("scan", u)])
            vector.tensor_copy(out=hst[:, g * DS:(g + 1) * DS],
                               in_=h_sb[:, TC - 1::TC]).then_inc(cv.h, 1)
            vector.wait_ge(cv.h, L[("hst", u)])
            ins = None
            for s2 in range(0, DS, 2):
                ins = vector.tensor_mul(
                    out=h_sb[:, s2 * TC:(s2 + 2) * TC],
                    in0=h_sb[:, s2 * TC:(s2 + 2) * TC],
                    in1=ct_t[ch % 2][:, s2 * TC:(s2 + 2) * TC])
            ins.then_inc(cv.h, 1)
            vector.wait_ge(cv.h, L[("rmult", u)])
            vector.tensor_add(out=h_sb[:, 0:2 * TC], in0=h_sb[:, 0:2 * TC],
                              in1=h_sb[:, 2 * TC:4 * TC])
            vector.tensor_add(out=h_sb[:, 4 * TC:6 * TC],
                              in0=h_sb[:, 4 * TC:6 * TC],
                              in1=h_sb[:, 6 * TC:8 * TC]).then_inc(cv.h, 1)
            vector.wait_ge(cv.h, L[("tree1", u)])
            vector.tensor_add(out=h_sb[:, 8 * TC:10 * TC],
                              in0=h_sb[:, 8 * TC:10 * TC],
                              in1=h_sb[:, 10 * TC:12 * TC])
            vector.tensor_add(out=h_sb[:, 12 * TC:14 * TC],
                              in0=h_sb[:, 12 * TC:14 * TC],
                              in1=h_sb[:, 14 * TC:16 * TC]).then_inc(cv.h, 1)
            vector.wait_ge(cv.h, L[("tree2", u)])
            vector.tensor_add(out=h_sb[:, 0:2 * TC], in0=h_sb[:, 0:2 * TC],
                              in1=h_sb[:, 4 * TC:6 * TC])
            vector.tensor_add(out=h_sb[:, 8 * TC:10 * TC],
                              in0=h_sb[:, 8 * TC:10 * TC],
                              in1=h_sb[:, 12 * TC:14 * TC]).then_inc(cv.h, 1)
            vector.wait_ge(cv.h, L[("tree3", u)])
            vector.tensor_add(out=h_sb[:, 0:2 * TC], in0=h_sb[:, 0:2 * TC],
                              in1=h_sb[:, 8 * TC:10 * TC]).then_inc(cv.h, 1)
            vector.wait_ge(cv.h, L[("comb", u)])
            vector.tensor_add(out=y_sb[:], in0=h_sb[:, 0:TC],
                              in1=h_sb[:, TC:2 * TC]).then_inc(cv.h, 1)
            vector.wait_ge(cv.h, L[("yfold", u)])
            vector.scalar_tensor_tensor(
                out=y_sb[:], in0=xcg_t[u % 4][:],
                scalar=dskip_s[:, g:g + 1], in1=y_sb[:],
                op0=ALU.mult, op1=ALU.add).then_inc(cv.h, 1)
            vector.wait_ge(cv.h, L[("stt", u)])
            vector.wait_ge(s_zg[u % 4].h, 16 * (u // 4 + 1))
            if ch >= 1:
                vector.wait_ge(s_y2[g].h, 16 * ch)
            vector.tensor_mul(out=y2b_t[g][:], in0=y_sb[:],
                              in1=zg_t[u % 4][:]).then_inc(cv.h, 1)

        @blk.vector
        def _(vector):
            for u in range(NU):
                emit_head(vector, u)
                if u >= 2:
                    emit_tail(vector, u - 2)
            emit_tail(vector, NU - 2)
            emit_tail(vector, NU - 1)
        cv.c += VI * NU

        s_yl = sem()
        s_ag = [sem() for _ in range(4)]

        @blk.gpsimd
        def _(gpsimd):
            for q in range(4):
                for g in range(NG):
                    gpsimd.wait_ge(s_y2[g].h, 16 * 8 * (q + 1))
                s_ag[q].inc(gpsimd.collective_compute(
                    "AllGather", ALU.bypass, replica_groups=RG,
                    ins=[y2_own[q].opt()], outs=[y2_all[q].opt()]))
                cc.c += 0
        CC_Y2 = cc.c

        barrier(extra=(s_y2[0], s_y2[1], s_y2[2], s_y2[3]))
        stD.close()

        stE = ExitStack()
        rhs2 = sb("rhs2", [128, 32 * TOK], BF16, stE)
        wst_e = [sb(f"wste{i}", [128, 512], F32, stE) for i in range(4)]
        wbf_e = [sb(f"wbfe{i}", [128, 512], BF16, stE) for i in range(4)]
        u0t = [sb(f"u0t{i}", [128, TOK], F32, stE) for i in range(2)]
        v_sb = sb("v_sb", [128, NFC * TOK], F32)
        ps_o = [psum(f"ps_o{i}", [128, TOK], stE) for i in range(4)]

        s_r2 = sem()
        s_we = [sem() for _ in range(4)]
        ce = sem()
        s_u0t = [sem() for _ in range(2)]

        e0_tn = tn.c
        e0_cv = cv.c
        e0_cs = cs.c

        @blk.sync
        def _(sync):
            for arm in nc.switch_core_id(engines=[sync], n=128):
                rk = arm.logical % W
                sync.wait_ge(s_ag[rk // 2].h, 1)
                co = (rk % 2) * TOK
                for k in range(32):
                    sync.dma_start(
                        out=rhs2[:, k * TOK:(k + 1) * TOK],
                        in_=y2_all[rk // 2][k * 128:(k + 1) * 128,
                                            co:co + TOK]).then_inc(s_r2.h, 16)
            s_r2.c += 16 * 32
            for mq in range(4):
                for k in range(32):
                    j = mq * 32 + k
                    if j >= 4:
                        sync.wait_ge(ce.h, (j - 4) + 1)
                    s_we[j % 4].inc(sync.dma_start(
                        out=wst_e[j % 4][:],
                        in_=outpT[k * 128:(k + 1) * 128,
                                  mq * 512:(mq + 1) * 512]), 16)

        @blk.scalar
        def _(scalar):
            for j in range(128):
                scalar.wait_ge(s_we[j % 4].h, 16 * (j // 4 + 1))
                if j >= 4:
                    scalar.wait_ge(tn.h, e0_tn + 4 * (j - 3))
                scalar.copy(out=wbf_e[j % 4][:],
                            in_=wst_e[j % 4][:]).then_inc(ce.h, 1)
                if j % 32 == 31:
                    mq = j // 32
                    scalar.wait_ge(tn.h, e0_tn + 128 * (mq + 1))
                    for m in range(4):
                        fc = mq * 4 + m
                        cs.inc(scalar.copy(
                            out=v_sb[:, fc * TOK:(fc + 1) * TOK],
                            in_=ps_o[m][:]))
        ce.c += 128

        @blk.tensor
        def _(tensor):
            tensor.wait_ge(s_r2.h, s_r2.c)
            for mq in range(4):
                if mq >= 1:
                    tensor.wait_ge(cs.h, e0_cs + 4 * mq)
                for k in range(32):
                    j = mq * 32 + k
                    tensor.wait_ge(ce.h, j + 1)
                    ins = None
                    for m in range(4):
                        ins = mm512(tensor, ps_o[m],
                                    wbf_e[j % 4][:, m * 128:(m + 1) * 128],
                                    rhs2[:, k * TOK:(k + 1) * TOK],
                                    start=(k == 0), stop=(k == 31))
                    tn.inc(ins, 4)

        e1_cv = cv.c

        @blk.sync
        def _(sync):
            for fc in range(NFC):
                if fc >= 2:
                    sync.wait_ge(cv.h, e1_cv + (fc - 2) + 1)
                s_u0t[fc % 2].inc(sync.dma_start(
                    out=u0t[fc % 2][:],
                    in_=u0_sp[fc * 128:(fc + 1) * 128, :]), 16)

        @blk.vector
        def _(vector):
            for fc in range(NFC):
                vector.wait_ge(cs.h, e0_cs + fc + 1)
                vector.wait_ge(s_u0t[fc % 2].h, 16 * (fc // 2 + 1))
                vector.tensor_add(out=v_sb[:, fc * TOK:(fc + 1) * TOK],
                                  in0=v_sb[:, fc * TOK:(fc + 1) * TOK],
                                  in1=u0t[fc % 2][:]).then_inc(cv.h, 1)
        cv.c += NFC

        barrier()
        stE.close()

        stG = ExitStack()
        ffh_sb = sb("ffh_sb", [128, 32 * TOK], BF16, stG)
        a2b_sb = sb("a2b_sb", [128, NFC * TOK], BF16, stG)
        stF = ExitStack()
        pstF = psum("pstF", [128, TOK], stF)
        psrF = psum("psrF", [128, TOK], stF)
        psqF = psum("psqF", [128, TOK], stF)
        scrF = (sb("sq0F", [128, TOK], F32, stF), sb("sq1F", [128, TOK], F32, stF),
                sb("mF", [1, TOK], F32, stF), sb("rF", [1, TOK], F32, stF),
                sb("qF", [1, TOK], F32, stF), sb("t1F", [128, TOK], F32, stF))
        emit_ln(lambda fc: v_sb[:, fc * TOK:(fc + 1) * TOK],
                lambda fc: a2b_sb[:, fc * TOK:(fc + 1) * TOK],
                ln2w_s, ln2b_s, (pstF, psrF, psqF), scrF,
                pre=((s_par, s_par.c),))
        barrier()
        stF.close()

        stG1 = ExitStack()
        wstg_p = [sb(f"wsp{i}", [128, NFC * 128], F32, stG1) for i in range(2)]
        wbf_a = [sb(f"wba{i}", [128, NFC * 128], BF16, stG1) for i in range(2)]
        wbf_g = [sb(f"wbg{i}", [128, NFC * 128], BF16, stG1) for i in range(2)]
        sg_sb = sb("sg_sb", [128, TOK], F32, stG1)
        ps_a = psum("ps_a", [128, TOK], stG1)
        ps_g = psum("ps_g", [128, TOK], stG1)

        s_wa = [sem() for _ in range(2)]
        s_wg = [sem() for _ in range(2)]
        ca = sem()
        g0_tn = tn.c
        g0_cs = cs.c
        g0_cv = cv.c

        @blk.sync
        def _(sync):
            for hc in range(32):
                if hc >= 1:
                    sync.wait_ge(ca.h, 2 * (hc - 1) + 1)
                s_wa[0].inc(sync.dma_start(
                    out=wstg_p[0][:].rearrange("p (k m) -> p k m", m=128),
                    in_=ffpT[:, hc * 128:(hc + 1) * 128].rearrange(
                        "(k p) m -> p k m", p=128)), 16)
                if hc >= 1:
                    sync.wait_ge(ca.h, 2 * (hc - 1) + 2)
                s_wg[0].inc(sync.dma_start(
                    out=wstg_p[1][:].rearrange("p (k m) -> p k m", m=128),
                    in_=ffpT[:, HH + hc * 128:HH + (hc + 1) * 128].rearrange(
                        "(k p) m -> p k m", p=128)), 16)

        @blk.scalar
        def _(scalar):
            for hc in range(33):
                if hc < 32:
                    scalar.wait_ge(s_wa[0].h, 16 * (hc + 1))
                    if hc >= 2:
                        scalar.wait_ge(tn.h, g0_tn + 2 * (hc - 1))
                    scalar.copy(out=wbf_a[hc % 2][:],
                                in_=wstg_p[0][:]).then_inc(ca.h, 1)
                    scalar.wait_ge(s_wg[0].h, 16 * (hc + 1))
                    scalar.copy(out=wbf_g[hc % 2][:],
                                in_=wstg_p[1][:]).then_inc(ca.h, 1)
                if hc >= 1:
                    h = hc - 1
                    scalar.wait_ge(tn.h, g0_tn + 2 * h + 2)
                    if h >= 1:
                        scalar.wait_ge(cv.h, g0_cv + h)
                    cs.inc(scalar.activation(out=sg_sb[:], in_=ps_g[:],
                                             func=AF.Silu))
        ca.c += 64

        @blk.vector
        def _(vector):
            for hc in range(32):
                vector.wait_ge(cs.h, g0_cs + hc + 1)
                vector.tensor_mul(out=ffh_sb[:, hc * TOK:(hc + 1) * TOK],
                                  in0=ps_a[:], in1=sg_sb[:]).then_inc(cv.h, 1)
        cv.c += 32

        @blk.tensor
        def _(tensor):
            for hc in range(32):
                tensor.wait_ge(ca.h, 2 * hc + 2)
                if hc >= 1:
                    tensor.wait_ge(cv.h, g0_cv + hc)
                for k in range(NFC):
                    mm512(tensor, ps_a, wbf_a[hc % 2][:, k * 128:(k + 1) * 128],
                          a2b_sb[:, k * TOK:(k + 1) * TOK],
                          start=(k == 0), stop=(k == NFC - 1))
                ins = None
                for k in range(NFC):
                    ins = mm512(tensor, ps_g,
                                wbf_g[hc % 2][:, k * 128:(k + 1) * 128],
                                a2b_sb[:, k * TOK:(k + 1) * TOK],
                                start=(k == 0), stop=(k == NFC - 1))
                tn.inc(ins, 2)

        barrier()
        stG1.close()

        stG2 = ExitStack()
        ffo_t = [sb(f"ffot{i}", [128, TOK], F32, stG2) for i in range(2)]
        xr_t = [sb(f"xrt{i}", [128, TOK], F32, stG2) for i in range(2)]
        wst_o = [sb(f"wsto{i}", [128, 512], F32, stG2) for i in range(4)]
        wbf_o = [sb(f"wbfo{i}", [128, 512], BF16, stG2) for i in range(4)]
        ps_f = [psum(f"ps_f{i}", [128, TOK], stG2) for i in range(4)]

        s_wo = [sem() for _ in range(4)]
        s_xr = [sem() for _ in range(2)]
        po = sem()
        g1_tn = tn.c
        g1_cv = cv.c
        g1_cs = cs.c

        @blk.sync
        def _(sync):
            for mq in range(4):
                for k in range(32):
                    j = mq * 32 + k
                    if j >= 4:
                        sync.wait_ge(cv.h, g1_cv + (j - 4) + 1)
                    s_wo[j % 4].inc(sync.dma_start(
                        out=wst_o[j % 4][:],
                        in_=ffoT[k * 128:(k + 1) * 128,
                                 mq * 512:(mq + 1) * 512]), 16)

        @blk.scalar
        def _(scalar):
            for j in range(128):
                scalar.wait_ge(s_wo[j % 4].h, 16 * (j // 4 + 1))
                if j >= 4:
                    scalar.wait_ge(tn.h, g1_tn + 4 * (j - 3))
                scalar.copy(out=wbf_o[j % 4][:],
                            in_=wst_o[j % 4][:]).then_inc(cv.h, 1)
                if j % 32 == 31:
                    mq = j // 32
                    scalar.wait_ge(tn.h, g1_tn + 128 * (mq + 1))
                    for m in range(4):
                        fc = mq * 4 + m
                        if fc >= 2:
                            scalar.wait_ge(po.h, 2 * (fc - 2) + 1)
                        cs.inc(scalar.copy(out=ffo_t[fc % 2][:],
                                           in_=ps_f[m][:]))
        cv.c += 128

        @blk.tensor
        def _(tensor):
            for mq in range(4):
                if mq >= 1:
                    tensor.wait_ge(cs.h, g1_cs + 4 * mq)
                for k in range(32):
                    j = mq * 32 + k
                    tensor.wait_ge(cv.h, g1_cv + j + 1)
                    ins = None
                    for m in range(4):
                        ins = mm512(tensor, ps_f[m],
                                    wbf_o[j % 4][:, m * 128:(m + 1) * 128],
                                    ffh_sb[:, k * TOK:(k + 1) * TOK],
                                    start=(k == 0), stop=(k == 31))
                    tn.inc(ins, 4)

        @blk.gpsimd
        def _(gpsimd):
            for fc in range(NFC):
                if fc >= 2:
                    gpsimd.wait_ge(po.h, 2 * (fc - 2) + 2)
                s_xr[fc % 2].inc(gpsimd.dma_start(
                    out=xr_t[fc % 2][:],
                    in_=xT[fc * 128:(fc + 1) * 128, :]), 16)
                gpsimd.wait_ge(cs.h, g1_cs + fc + 1)
                gpsimd.tensor_add(out=v_sb[:, fc * TOK:(fc + 1) * TOK],
                                  in0=v_sb[:, fc * TOK:(fc + 1) * TOK],
                                  in1=ffo_t[fc % 2][:]).then_inc(po.h, 1)
                gpsimd.wait_ge(po.h, 2 * fc + 1)
                gpsimd.wait_ge(s_xr[fc % 2].h, 16 * (fc // 2 + 1))
                gpsimd.tensor_add(out=v_sb[:, fc * TOK:(fc + 1) * TOK],
                                  in0=v_sb[:, fc * TOK:(fc + 1) * TOK],
                                  in1=xr_t[fc % 2][:]).then_inc(po.h, 1)
                gpsimd.wait_ge(po.h, 2 * fc + 2)
        po.c += 2 * NFC
        x1_sb = v_sb

        barrier()
        stG2.close()
        stG.close()

        stH = ExitStack()
        u1b_sb = sb("u1b_sb", [128, NFC * TOK], BF16)
        pstH = psum("pstH", [128, TOK], stH)
        psrH = psum("psrH", [128, TOK], stH)
        psqH = psum("psqH", [128, TOK], stH)
        scrH = (sb("sq0H", [128, TOK], F32, stH), sb("sq1H", [128, TOK], F32, stH),
                sb("mH", [1, TOK], F32, stH), sb("rH", [1, TOK], F32, stH),
                sb("qH", [1, TOK], F32, stH), sb("t1H", [128, TOK], F32, stH))
        emit_ln(lambda fc: x1_sb[:, fc * TOK:(fc + 1) * TOK],
                lambda fc: u1b_sb[:, fc * TOK:(fc + 1) * TOK],
                None, None, (pstH, psrH, psqH), scrH)
        barrier()
        stH.close()

        stI = ExitStack()
        wbf_i = [sb(f"wbfi{i}", [128, 512], BF16, stI) for i in range(4)]
        out_t = [sb(f"outt{i}", [128, TOK], F32, stI) for i in range(2)]
        ps_i = [psum(f"ps_i{i}", [128, TOK], stI) for i in range(4)]

        s_wi = [sem() for _ in range(4)]
        so_p = [sem() for _ in range(2)]
        i0_tn = tn.c
        i0_cs = cs.c
        i0_cv = cv.c

        @blk.sync
        def _(sync):
            sync.wait_ge(cc.h, CC_FS)
            for mq in range(4):
                for k in range(NFC):
                    j = mq * NFC + k
                    if j >= 4:
                        sync.wait_ge(tn.h, i0_tn + 4 * (j - 3))
                        sync.wait_ge(s_wi[j % 4].h, 64 * ((j - 4) // 4 + 1))
                    ins = None
                    for m in range(4):
                        e = mq * 4 + m
                        r = e // 2
                        col = (e % 2) * 128
                        ins = sync.dma_start(
                            out=wbf_i[j % 4][:, m * 128:(m + 1) * 128],
                            in_=fs_all[r * D + k * 128:r * D + (k + 1) * 128,
                                       col:col + 128])
                        s_wi[j % 4].inc(ins, 16)

        @blk.tensor
        def _(tensor):
            for mq in range(4):
                if mq >= 1:
                    tensor.wait_ge(cs.h, i0_cs + 4 * mq)
                for k in range(NFC):
                    j = mq * NFC + k
                    tensor.wait_ge(s_wi[j % 4].h, 64 * (j // 4 + 1))
                    ins = None
                    for m in range(4):
                        ins = mm512(tensor, ps_i[m],
                                    wbf_i[j % 4][:, m * 128:(m + 1) * 128],
                                    u1b_sb[:, k * TOK:(k + 1) * TOK],
                                    start=(k == 0), stop=(k == NFC - 1))
                    tn.inc(ins, 4)

        @blk.scalar
        def _(scalar):
            scalar.wait_ge(s_par.h, s_par.c)
            for mq in range(4):
                for m in range(4):
                    fc = mq * 4 + m
                    scalar.wait_ge(tn.h, i0_tn + 64 * (mq + 1))
                    if fc >= 2:
                        scalar.wait_ge(so_p[fc % 2].h, 16 * (fc // 2))
                    cs.inc(scalar.activation(
                        out=out_t[fc % 2][:], in_=ps_i[m][:], func=AF.Identity,
                        scale=RSQRT8, bias=fbias_s[:, fc:fc + 1]))

        @blk.vector
        def _(vector):
            for fc in range(NFC):
                vector.wait_ge(cs.h, i0_cs + fc + 1)
                vector.tensor_add(out=out_t[fc % 2][:],
                                  in0=out_t[fc % 2][:],
                                  in1=x1_sb[:, fc * TOK:(fc + 1) * TOK]
                                  ).then_inc(cv.h, 1)
        cv.c += NFC

        @blk.gpsimd
        def _(gpsimd):
            for fc in range(NFC):
                gpsimd.wait_ge(cv.h, i0_cv + fc + 1)
                so_p[fc % 2].inc(gpsimd.dma_start(
                    out=out_fm[fc * 128:(fc + 1) * 128, :],
                    in_=out_t[fc % 2][:]), 16)
            gpsimd.wait_ge(so_p[0].h, so_p[0].c)
            gpsimd.wait_ge(so_p[1].h, so_p[1].c)

        barrier()
        stI.close()

    st.close()
    return nc


_NC_CACHE = {}


def kernel(**inputs):
    f32 = np.float32

    def cac(a):
        return np.ascontiguousarray(a, dtype=f32)

    x = np.asarray(inputs["L0_attn_x"])[0]
    xT = cac(x.T)
    w_in = np.asarray(inputs["L0_attn_in_proj_w"])
    wInXT = cac(w_in[:DI].T)
    wInZT = cac(w_in[DI:].T)
    conv_w = np.asarray(inputs["L0_attn_conv_w"])[:, 0, :]
    conv_b = np.asarray(inputs["L0_attn_conv_b"])
    xpT_f = cac(np.asarray(inputs["L0_attn_x_proj_w"]).T)
    dtpT_f = cac(np.asarray(inputs["L0_attn_dt_proj_w"]).T)
    dt_b = np.asarray(inputs["L0_attn_dt_proj_b"])
    A_log = np.asarray(inputs["L0_attn_A_log"])
    Dskip = np.asarray(inputs["L0_attn_D_skip"])
    outpT = cac(np.asarray(inputs["L0_attn_out_proj_w"]).T)
    ffpT = cac(np.asarray(inputs["L0_attn_ff_proj_w"]).T)
    ffoT = cac(np.asarray(inputs["L0_attn_ff_out_w"]).T)
    factors = np.asarray(inputs["L0_ffn_factors"])

    def col16(vec):
        return cac(np.asarray(vec).reshape(NFC, 128).T)

    def colg(vec):
        return cac(np.asarray(vec).reshape(NG, 128).T)

    ln1w = col16(inputs["L0_attn_ln1_w"]); ln1b = col16(inputs["L0_attn_ln1_b"])
    ln2w = col16(inputs["L0_attn_ln2_w"]); ln2b = col16(inputs["L0_attn_ln2_b"])
    fbias_t = col16(inputs["L0_ffn_bias"])

    in_maps = []
    for c in range(W):
        dsl = slice(c * DIC, (c + 1) * DIC)
        in_maps.append({
            "xT": cac(xT[:, c * TOK:(c + 1) * TOK]),
            "ln1w": ln1w, "ln1b": ln1b, "ln2w": ln2w, "ln2b": ln2b,
            "fbias": fbias_t,
            "wInXT": cac(wInXT[:, dsl]),
            "wInZT": cac(wInZT[:, dsl]),
            "convw": cac(conv_w[dsl].reshape(NG, 128, 4).transpose(1, 0, 2)
                         .reshape(128, 4 * NG)),
            "convb": colg(conv_b[dsl]),
            "dskip": colg(Dskip[dsl]),
            "dtb": colg(dt_b[dsl]),
            "alog": cac(A_log[dsl].reshape(NG, 128, DS).transpose(1, 0, 2)
                        .reshape(128, DS * NG)),
            "xpT": cac(xpT_f[dsl].reshape(NG, 128, 160).transpose(1, 0, 2)
                       .reshape(128, 160 * NG)),
            "dtpT": cac(dtpT_f[:, dsl]),
            "outpT": outpT,
            "ffpT": ffpT,
            "ffoT": ffoT,
            "fct": cac(factors[:, :, c * (D // W):(c + 1) * (D // W)]
                       .reshape(8 * D, D // W)),
            "consts": np.tile(np.array([[EPS, 1.0, -1.0, RSQRT8]], f32),
                              (128, 1)),
        })

    if "nc" not in _NC_CACHE:
        _NC_CACHE["nc"] = build_nc()
    nc = _NC_CACHE["nc"]
    res = run_bass_kernel_spmd(nc, in_maps, core_ids=list(range(W)))
    out = np.concatenate(
        [np.asarray(res.results[c]["out_fm"]).T for c in range(W)], axis=0)
    return np.ascontiguousarray(out[None], dtype=np.float32)


# version 17
# speedup vs baseline: 1.0169x; 1.0024x over previous
import numpy as np
from contextlib import ExitStack

import concourse.bass as bass
import concourse.mybir as mybir
from concourse.bass_utils import run_bass_kernel_spmd

F32 = mybir.dt.float32
BF16 = mybir.dt.bfloat16
AF = mybir.ActivationFunctionType
ALU = mybir.AluOpType
AX = mybir.AxisListType

W = 8
T = 8192
TOK = T // W
D = 2048
NFC = D // 128
DI = 4096
DIC = DI // W
NG = DIC // 128
DS = 16
DTR = 128
HH = 4096
TC = 256
NCH = T // TC
NU = NCH * NG
RG = [list(range(W))]
EPS = 1e-5
RSQRT8 = float(1.0 / np.sqrt(8.0))


class Sem:

    def __init__(self, handle):
        self.h = handle
        self.c = 0

    def inc(self, instr, k=1):
        instr.then_inc(self.h, k)
        self.c += k
        return self.c


def mm512(tensor, out, lhsT, rhs, start, stop, n=TOK):
    ins = None
    for c0 in range(0, n, 512):
        ins = tensor.matmul(out=out[:, c0:c0 + 512], lhsT=lhsT,
                            rhs=rhs[:, c0:c0 + 512], start=start, stop=stop)
    return ins


def build_nc():
    nc = bass.Bass("TRN2", target_bir_lowering=False)

    def inp(name, shape):
        return nc.dram_tensor(name, list(shape), F32, kind="ExternalInput").ap()

    xT = inp("xT", [D, TOK])
    ln1w = inp("ln1w", [128, NFC]); ln1b = inp("ln1b", [128, NFC])
    ln2w = inp("ln2w", [128, NFC]); ln2b = inp("ln2b", [128, NFC])
    fbias = inp("fbias", [128, NFC])
    wInXT = inp("wInXT", [D, DIC])
    wInZT = inp("wInZT", [D, DIC])
    convw = inp("convw", [128, 4 * NG])
    convb = inp("convb", [128, NG])
    dskip = inp("dskip", [128, NG])
    dtb = inp("dtb", [128, NG])
    alog = inp("alog", [128, DS * NG])
    xpT = inp("xpT", [128, 160 * NG])
    dtpT = inp("dtpT", [DTR, DIC])
    outpT = inp("outpT", [DI, D])
    ffpT = inp("ffpT", [D, 2 * HH])
    ffoT = inp("ffoT", [HH, D])
    fct = inp("fct", [8 * D, D // W])
    consts = inp("consts", [128, 4])

    out_fm = nc.dram_tensor("out_fm", [D, TOK], F32, kind="ExternalOutput").ap()

    def idram(name, shape, dt, shared=False):
        kw = dict(addr_space="Shared") if shared else {}
        return nc.dram_tensor(name, list(shape), dt, **kw).ap()

    a1b_own = idram("a1b_own", [D, TOK], BF16)
    a1b_all = idram("a1b_all", [W * D, TOK], BF16, shared=True)
    xdbl_part = idram("xdbl_part", [160, T], F32)
    xdbl = idram("xdbl", [160, T], F32, shared=True)
    xc_sp = idram("xc_sp", [DIC, T], F32)
    z_sp = idram("z_sp", [DIC, T], F32)
    u0_sp = idram("u0_sp", [D, TOK], F32)
    y2_own = [idram(f"y2_own{q}", [DIC, T // 4], BF16) for q in range(4)]
    y2_all = [idram(f"y2_all{q}", [DI, T // 4], BF16, shared=True)
              for q in range(4)]
    y2_loc = idram("y2_loc", [DI, TOK], BF16)
    fs_own = idram("fs_own", [D, D // W], BF16)
    fs_all = idram("fs_all", [W * D, D // W], BF16, shared=True)

    st = ExitStack()

    def sb(name, shape, dt, stk=None):
        side = "left" if stk is None else "right"
        return (stk or st).enter_context(
            nc.sbuf_tensor(name, list(shape), dt, side=side))

    def psum(name, shape, stk=None):
        return (stk or st).enter_context(nc.psum_tensor(name, list(shape), F32))

    _sem_i = [0]

    def sem():
        _sem_i[0] += 1
        return Sem(st.enter_context(nc.semaphore(f"s{_sem_i[0]}")))

    ones = sb("ones", [128, 128], F32)
    ln1w_s = sb("ln1w_s", [128, NFC], F32); ln1b_s = sb("ln1b_s", [128, NFC], F32)
    ln2w_s = sb("ln2w_s", [128, NFC], F32); ln2b_s = sb("ln2b_s", [128, NFC], F32)
    fbias_s = sb("fbias_s", [128, NFC], F32)
    convw_s = sb("convw_s", [128, 4 * NG], F32)
    convb_s = sb("convb_s", [128, NG], F32)
    dskip_s = sb("dskip_s", [128, NG], F32)
    dtb_s = sb("dtb_s", [128, NG], F32)
    a_s = sb("a_s", [128, DS * NG], F32)
    xp_bf = sb("xp_bf", [128, 160 * NG], BF16)
    dtp_bf = sb("dtp_bf", [DTR, DIC], BF16)
    consts_s = sb("consts_s", [128, 4], F32)
    for i, val in enumerate((EPS, 1.0, -1.0, RSQRT8)):
        nc.const_aps.register(F32, float(val), consts_s[:, i:i + 1])

    ph = sem()
    cc = sem()
    tn = sem()
    cv = sem()
    cs = sem()
    so = sem()
    ld = sem()

    ENG = ("sync", "vector", "scalar", "tensor", "gpsimd")
    bh = {}

    def barrier(extra=()):
        blk = bh["blk"]
        extra = [(s_, s_.c) for s_ in extra]
        target = ph.c + len(ENG)
        ph.c = target
        for e in ENG:
            def body(eng):
                eng.drain()
                eng.sem_inc(ph.h, 1)
                eng.wait_ge(ph.h, target)
                for s_, v_ in extra:
                    eng.wait_ge(s_.h, v_)
            getattr(blk, e)(body)

    def emit_ln(src_fn, dst_fn, w_s, b_s, lnp, scr, pre=()):
        blk = bh["blk"]
        pst, psr, psq = lnp
        sq0, sq1, m_row, r_row, q_row, t1 = scr
        sqs = (sq0, sq1)
        cs0, tn0 = cs.c, tn.c

        @blk.scalar
        def _(scalar):
            for s_, v_ in pre:
                scalar.wait_ge(s_.h, v_)
            for fc in range(NFC):
                if fc >= 2:
                    scalar.wait_ge(tn.h, tn0 + fc - 1)
                cs.inc(scalar.activation(out=sqs[fc % 2][:], in_=src_fn(fc),
                                         func=AF.Square))

        @blk.tensor
        def _(tensor):
            for s_, v_ in pre:
                tensor.wait_ge(s_.h, v_)
            for fc in range(NFC):
                for hf in range(2):
                    sl = slice(hf * 512, (hf + 1) * 512)
                    tensor.matmul(out=pst[0:1, sl], lhsT=ones[:, 0:1],
                                  rhs=src_fn(fc)[:, sl],
                                  start=(fc == 0), stop=(fc == NFC - 1))
                tensor.wait_ge(cs.h, cs0 + fc + 1)
                ins = None
                for hf in range(2):
                    sl = slice(hf * 512, (hf + 1) * 512)
                    ins = tensor.matmul(out=pst[32:33, sl], lhsT=ones[:, 0:1],
                                        rhs=sqs[fc % 2][:, sl],
                                        start=(fc == 0), stop=(fc == NFC - 1))
                tn.inc(ins)

        cv0 = cv.c

        @blk.vector
        def _(vector):
            vector.wait_ge(tn.h, tn.c)
            vector.tensor_scalar(out=m_row[:], in0=pst[0:1, :], scalar1=1.0 / D,
                                 scalar2=None, op0=ALU.mult).then_inc(cv.h, 1)
            vector.wait_ge(cv.h, cv0 + 1)
            vector.tensor_scalar(out=r_row[:], in0=pst[32:33, :], scalar1=1.0 / D,
                                 scalar2=None, op0=ALU.mult).then_inc(cv.h, 1)
            vector.wait_ge(cv.h, cv0 + 2)
            vector.tensor_mul(out=q_row[:], in0=m_row[:],
                              in1=m_row[:]).then_inc(cv.h, 1)
            vector.wait_ge(cv.h, cv0 + 3)
            vector.tensor_sub(out=r_row[:], in0=r_row[:],
                              in1=q_row[:]).then_inc(cv.h, 1)
        cv.c += 4

        @blk.scalar
        def _(scalar):
            scalar.wait_ge(cv.h, cv.c)
            cs.inc(scalar.activation(out=q_row[:], in_=r_row[:], func=AF.Sqrt,
                                     bias=EPS))

        cv1 = cv.c

        @blk.vector
        def _(vector):
            vector.wait_ge(cs.h, cs.c)
            vector.reciprocal(out=r_row[:], in_=q_row[:]).then_inc(cv.h, 1)
            vector.wait_ge(cv.h, cv1 + 1)
            vector.tensor_mul(out=q_row[:], in0=m_row[:],
                              in1=r_row[:]).then_inc(cv.h, 1)
            vector.wait_ge(cv.h, cv1 + 2)
            vector.tensor_scalar(out=q_row[:], in0=q_row[:], scalar1=-1.0,
                                 scalar2=None, op0=ALU.mult).then_inc(cv.h, 1)
        cv.c += 3

        @blk.tensor
        def _(tensor):
            tensor.wait_ge(cv.h, cv.c)
            for half in range(2):
                sl = slice(half * 512, (half + 1) * 512)
                tensor.matmul(out=psr[:, sl], lhsT=ones[0:1, :], rhs=r_row[:, sl],
                              start=True, stop=True)
                tn.inc(tensor.matmul(out=psq[:, sl], lhsT=ones[0:1, :],
                                     rhs=q_row[:, sl], start=True, stop=True))

        cv2 = cv.c

        @blk.vector
        def _(vector):
            vector.wait_ge(tn.h, tn.c)
            for fc in range(NFC):
                vector.tensor_mul(out=t1[:], in0=src_fn(fc),
                                  in1=psr[:]).then_inc(cv.h, 1)
                vector.wait_ge(cv.h, cv2 + 3 * fc + 1)
                if w_s is None:
                    vector.tensor_add(out=dst_fn(fc), in0=t1[:],
                                      in1=psq[:]).then_inc(cv.h, 2)
                    vector.wait_ge(cv.h, cv2 + 3 * fc + 3)
                else:
                    vector.tensor_add(out=t1[:], in0=t1[:],
                                      in1=psq[:]).then_inc(cv.h, 1)
                    vector.wait_ge(cv.h, cv2 + 3 * fc + 2)
                    vector.tensor_scalar(
                        out=dst_fn(fc), in0=t1[:],
                        scalar1=w_s[:, fc:fc + 1], scalar2=b_s[:, fc:fc + 1],
                        op0=ALU.mult, op1=ALU.add).then_inc(cv.h, 1)
                    vector.wait_ge(cv.h, cv2 + 3 * fc + 3)
        cv.c += 3 * NFC

    with nc.Block() as blk:
        bh["blk"] = blk

        stA = ExitStack()
        stAB = ExitStack()
        stABC = ExitStack()
        wInX_bf = sb("wInX_bf", [128, NFC * 512], BF16, stABC)
        wInZ_bf = sb("wInZ_bf", [128, NFC * 512], BF16, stABC)
        x_sb = sb("x_sb", [128, NFC * TOK], F32, stAB)
        wstg = sb("wstg", [128, NFC * 512], F32, stA)
        fstg = sb("fstg", [128, NFC * 256], F32, stA)
        facc = sb("facc", [128, NFC * 256], F32, stA)
        fbf = sb("fbf", [128, NFC * 256], BF16, stA)
        xp_st = sb("xp_st", [128, 160 * NG], F32, stA)
        dtp_st = sb("dtp_st", [DTR, DIC], F32, stA)
        ascr = sb("ascr", [128, DS * NG], F32, stA)

        s_par = sem()
        s_wx = sem()
        s_wz = sem()
        s_fc = [sem() for _ in range(2)]
        s_fs = sem()

        @blk.sync
        def _(sync):
            s_par.inc(sync.dma_start(
                out=x_sb[:].rearrange("p (f t) -> p f t", t=TOK),
                in_=xT.rearrange("(f p) t -> p f t", p=128)), 16)
            for t_, d_ in ((ln1w, ln1w_s), (ln1b, ln1b_s), (ln2w, ln2w_s),
                           (ln2b, ln2b_s), (fbias, fbias_s), (convw, convw_s),
                           (convb, convb_s), (dskip, dskip_s), (dtb, dtb_s),
                           (alog, a_s), (xpT, xp_st), (dtpT, dtp_st),
                           (consts, consts_s)):
                s_par.inc(sync.dma_start(out=d_[:], in_=t_), 16)
            s_wx.inc(sync.dma_start(
                out=wstg[:].rearrange("p (k m) -> p k m", m=512),
                in_=wInXT.rearrange("(k p) m -> p k m", p=128)), 16)

        cvA0 = cv.c

        @blk.vector
        def _(vector):
            vector.memset(ones[:], 1.0).then_inc(cv.h, 1)
            vector.wait_ge(s_par.h, s_par.c)
            vector.tensor_copy(out=xp_bf[:], in_=xp_st[:]).then_inc(cv.h, 1)
            vector.tensor_copy(out=dtp_bf[:], in_=dtp_st[:]).then_inc(cv.h, 1)
            vector.wait_ge(s_wx.h, s_wx.c)
            vector.tensor_copy(out=wInX_bf[:], in_=wstg[:]).then_inc(cv.h, 1)
        cv.c += 4

        @blk.sync
        def _(sync):
            sync.wait_ge(cv.h, cv.c)
            s_wz.inc(sync.dma_start(
                out=wstg[:].rearrange("p (k m) -> p k m", m=512),
                in_=wInZT.rearrange("(k p) m -> p k m", p=128)), 16)

        @blk.vector
        def _(vector):
            vector.wait_ge(s_wz.h, s_wz.c)
            vector.tensor_copy(out=wInZ_bf[:], in_=wstg[:]).then_inc(cv.h, 1)
        cv.c += 1

        @blk.scalar
        def _(scalar):
            scalar.wait_ge(s_par.h, s_par.c)
            cs.inc(scalar.activation(out=ascr[:], in_=a_s[:], func=AF.Exp))

        @blk.vector
        def _(vector):
            vector.wait_ge(cs.h, cs.c)
            vector.tensor_scalar(out=a_s[:], in0=ascr[:], scalar1=-1.0,
                                 scalar2=None, op0=ALU.mult).then_inc(cv.h, 1)
        cv.c += 1

        cvF0 = cv.c

        @blk.gpsimd
        def _(gpsimd):
            for r in range(8):
                if r >= 2:
                    gpsimd.wait_ge(cv.h, cvF0 + (r - 1))
                dst = facc if r == 0 else fstg
                s_fc[r % 2].inc(gpsimd.dma_start(
                    out=dst[:].rearrange("p (f e) -> p f e", e=256),
                    in_=fct[r * D:(r + 1) * D, :].rearrange(
                        "(f p) e -> p f e", p=128)), 16)

        fc_ms = {}
        cnt = [0, 0]
        for r in range(8):
            cnt[r % 2] += 16
            fc_ms[r] = (r % 2, cnt[r % 2])

        @blk.vector
        def _(vector):
            vector.wait_ge(s_fc[0].h, 16)
            for r in range(1, 8):
                p_, v_ = fc_ms[r]
                vector.wait_ge(s_fc[p_].h, v_)
                vector.wait_ge(cv.h, cvF0 + (r - 1))
                vector.tensor_add(out=facc[:], in0=facc[:],
                                  in1=fstg[:]).then_inc(cv.h, 1)
            vector.wait_ge(cv.h, cvF0 + 7)
            vector.tensor_copy(out=fbf[:], in_=facc[:]).then_inc(cv.h, 1)
        cv.c += 8

        @blk.sync
        def _(sync):
            sync.wait_ge(cv.h, cv.c)
            s_fs.inc(sync.dma_start(
                out=fs_own.rearrange("(f p) e -> p f e", p=128),
                in_=fbf[:].rearrange("p (f e) -> p f e", e=256)), 16)

        @blk.gpsimd
        def _(gpsimd):
            gpsimd.wait_ge(s_fs.h, s_fs.c)
            cc.inc(gpsimd.collective_compute(
                "AllGather", ALU.bypass, replica_groups=RG,
                ins=[fs_own.opt()], outs=[fs_all.opt()]))
        CC_FS = cc.c

        barrier(extra=(s_fs,))
        stA.close()

        stB = ExitStack()
        u0_sb = x_sb
        a1b_sb = sb("a1b_sb", [128, NFC * TOK], BF16, stB)
        pstB = psum("pstB", [128, TOK], stB)
        psrB = psum("psrB", [128, TOK], stB)
        psqB = psum("psqB", [128, TOK], stB)
        scrB = (sb("sq0", [128, TOK], F32, stB), sb("sq1", [128, TOK], F32, stB),
                sb("m_row", [1, TOK], F32, stB), sb("r_row", [1, TOK], F32, stB),
                sb("q_row", [1, TOK], F32, stB), sb("t1B", [128, TOK], F32, stB))

        emit_ln(lambda fc: x_sb[:, fc * TOK:(fc + 1) * TOK],
                lambda fc: u0_sb[:, fc * TOK:(fc + 1) * TOK],
                None, None, (pstB, psrB, psqB), scrB,
                pre=((s_par, s_par.c),))
        emit_ln(lambda fc: u0_sb[:, fc * TOK:(fc + 1) * TOK],
                lambda fc: a1b_sb[:, fc * TOK:(fc + 1) * TOK],
                ln1w_s, ln1b_s, (pstB, psrB, psqB), scrB,
                pre=((cv, cv.c), (s_par, s_par.c)))

        s_u0sp = sem()
        s_a1 = sem()

        @blk.sync
        def _(sync):
            sync.wait_ge(cv.h, cv.c)
            s_u0sp.inc(sync.dma_start(
                out=u0_sp.rearrange("(f p) t -> p f t", p=128),
                in_=u0_sb[:].rearrange("p (f t) -> p f t", t=TOK)), 16)
            s_a1.inc(sync.dma_start(
                out=a1b_own.rearrange("(f p) t -> p f t", p=128),
                in_=a1b_sb[:].rearrange("p (f t) -> p f t", t=TOK)), 16)

        @blk.gpsimd
        def _(gpsimd):
            gpsimd.wait_ge(s_a1.h, s_a1.c)
            cc.inc(gpsimd.collective_compute(
                "AllGather", ALU.bypass, replica_groups=RG,
                ins=[a1b_own.opt()], outs=[a1b_all.opt()]))
        CC_A1 = cc.c

        barrier(extra=(s_u0sp, s_a1))
        stB.close()
        stAB.close()

        stC = ExitStack()
        wInX_bf2 = wInX_bf
        wInZ_bf2 = wInZ_bf
        rhs_t = [sb(f"rhs{i}", [128, NFC * TOK], BF16, stC) for i in range(2)]
        ps_in = [psum(f"ps_in{i}", [128, TOK], stC) for i in range(2)]
        ps_xdt = psum("ps_xdt", [128, TOK], stC)
        ps_xbc = psum("ps_xbc", [32, TOK], stC)
        xm_t = [sb(f"xm{g}", [128, TOK + 3], F32, stC) for g in range(NG)]
        xc_t = [sb(f"xct{g}", [128, TOK], F32, stC) for g in range(NG)]
        xcb_t = [sb(f"xcb{g}", [128, TOK], BF16, stC) for g in range(NG)]
        z_t = [sb(f"zt{g}", [128, TOK], F32, stC) for g in range(NG)]
        xdt_ev = sb("xdt_ev", [128, TOK], F32, stC)
        xbc_ev = sb("xbc_ev", [32, TOK], F32, stC)

        s_rhs = [sem() for _ in range(2)]
        s_xcz = sem()
        s_xdb = sem()

        tc0_tn = tn.c
        tc0_cs = cs.c
        tc0_cv = cv.c

        def ev_ms(u):
            return tc0_cs + (u // 8) * 12 + (u % 8) + 1

        @blk.vector
        def _(vector):
            for g in range(NG):
                vector.memset(xm_t[g][:, 0:3], 0.0).then_inc(cv.h, 1)
        cv.c += NG
        tc0_cv = cv.c

        @blk.sync
        def _(sync):
            sync.wait_ge(cc.h, CC_A1)
            for tc in range(W):
                base = tc * D
                if tc >= 2:
                    sync.wait_ge(tn.h, tc0_tn + (tc - 2) * 10 + 8)
                for k in range(NFC):
                    s_rhs[tc % 2].inc(sync.dma_start(
                        out=rhs_t[tc % 2][:, k * TOK:(k + 1) * TOK],
                        in_=a1b_all[base + k * 128: base + (k + 1) * 128, :]), 16)

        @blk.tensor
        def _(tensor):
            for tc in range(W):
                tensor.wait_ge(s_rhs[tc % 2].h, 16 * NFC * (tc // 2 + 1))
                for mg in range(8):
                    u = tc * 8 + mg
                    if u >= 2:
                        tensor.wait_ge(cs.h, ev_ms(u - 2))
                    wsel = wInX_bf2 if mg < 4 else wInZ_bf2
                    msub = mg % 4
                    ins = None
                    for k in range(NFC):
                        ins = mm512(tensor, ps_in[u % 2],
                                    wsel[:, k * 512 + msub * 128:
                                         k * 512 + (msub + 1) * 128],
                                    rhs_t[tc % 2][:, k * TOK:(k + 1) * TOK],
                                    start=(k == 0), stop=(k == NFC - 1))
                    tn.inc(ins)
                tensor.wait_ge(cv.h, tc0_cv + tc * 26 + 24)
                if tc >= 1:
                    tensor.wait_ge(cv.h, tc0_cv + (tc - 1) * 26 + 26)
                ins = None
                for g in range(NG):
                    mm512(tensor, ps_xdt, xp_bf[:, g * 160:g * 160 + 128],
                          xcb_t[g][:], start=(g == 0), stop=(g == NG - 1))
                    ins = mm512(tensor, ps_xbc,
                                xp_bf[:, g * 160 + 128:g * 160 + 160],
                                xcb_t[g][:], start=(g == 0), stop=(g == NG - 1))
                tn.inc(ins, 2)

        @blk.scalar
        def _(scalar):
            for tc in range(W):
                for mg in range(8):
                    scalar.wait_ge(tn.h, tc0_tn + tc * 10 + mg + 1)
                    if mg < 4:
                        if tc >= 1:
                            scalar.wait_ge(cv.h,
                                           tc0_cv + (tc - 1) * 26 + mg * 5 + 5)
                        cs.inc(scalar.copy(out=xm_t[mg][:, 3:3 + TOK],
                                           in_=ps_in[(tc * 8 + mg) % 2][:]))
                    else:
                        if tc >= 1:
                            scalar.wait_ge(s_xcz.h, 128 * tc)
                        cs.inc(scalar.activation(
                            out=z_t[mg - 4][:], in_=ps_in[(tc * 8 + mg) % 2][:],
                            func=AF.Silu))
                for g in range(NG):
                    scalar.wait_ge(cv.h, tc0_cv + tc * 26 + g * 5 + 4)
                    cs.inc(scalar.activation(out=xc_t[g][:], in_=xc_t[g][:],
                                             func=AF.Silu,
                                             bias=convb_s[:, g:g + 1]))

        @blk.vector
        def _(vector):
            vector.wait_ge(s_par.h, s_par.c)
            for tc in range(W):
                cvb = tc0_cv + tc * 26
                for g in range(NG):
                    vector.wait_ge(cs.h, tc0_cs + tc * 12 + g + 1)
                    if tc >= 1:
                        vector.wait_ge(s_xcz.h, 128 * tc)
                        vector.wait_ge(cv.h, tc0_cv + (tc - 1) * 26 + g * 5 + 5)
                    else:
                        vector.wait_ge(cv.h, tc0_cv)
                    vector.tensor_scalar(
                        out=xc_t[g][:], in0=xm_t[g][:, 0:TOK],
                        scalar1=convw_s[:, g * 4:g * 4 + 1], scalar2=None,
                        op0=ALU.mult).then_inc(cv.h, 1)
                    for k in range(1, 4):
                        vector.wait_ge(cv.h, cvb + g * 5 + k)
                        vector.scalar_tensor_tensor(
                            out=xc_t[g][:], in0=xm_t[g][:, k:k + TOK],
                            scalar=convw_s[:, g * 4 + k:g * 4 + k + 1],
                            in1=xc_t[g][:], op0=ALU.mult,
                            op1=ALU.add).then_inc(cv.h, 1)
                    vector.wait_ge(cv.h, cvb + g * 5 + 4)
                    vector.tensor_copy(out=xm_t[g][:, 0:3],
                                       in_=xm_t[g][:, TOK:TOK + 3]
                                       ).then_inc(cv.h, 1)
                for g in range(NG):
                    vector.wait_ge(cs.h, tc0_cs + tc * 12 + 8 + g + 1)
                    vector.tensor_copy(out=xcb_t[g][:],
                                       in_=xc_t[g][:]).then_inc(cv.h, 1)
                vector.wait_ge(tn.h, tc0_tn + tc * 10 + 10)
                if tc >= 1:
                    vector.wait_ge(s_xdb.h, 32 * tc)
                vector.tensor_copy(out=xdt_ev[:], in_=ps_xdt[:]).then_inc(cv.h, 1)
                vector.tensor_copy(out=xbc_ev[:], in_=ps_xbc[:]).then_inc(cv.h, 1)
        cv.c += 26 * W

        @blk.gpsimd
        def _(gpsimd):
            for tc in range(W):
                for g in range(NG):
                    gpsimd.wait_ge(cs.h, tc0_cs + tc * 12 + 8 + g + 1)
                    s_xcz.inc(gpsimd.dma_start(
                        out=xc_sp[g * 128:(g + 1) * 128,
                                  tc * TOK:(tc + 1) * TOK],
                        in_=xc_t[g][:]), 16)
                    gpsimd.wait_ge(cs.h, tc0_cs + tc * 12 + 4 + g + 1)
                    s_xcz.inc(gpsimd.dma_start(
                        out=z_sp[g * 128:(g + 1) * 128,
                                 tc * TOK:(tc + 1) * TOK],
                        in_=z_t[g][:]), 16)
                gpsimd.wait_ge(cv.h, tc0_cv + tc * 26 + 26)
                s_xdb.inc(gpsimd.dma_start(
                    out=xdbl_part[0:128, tc * TOK:(tc + 1) * TOK],
                    in_=xdt_ev[:]), 16)
                s_xdb.inc(gpsimd.dma_start(
                    out=xdbl_part[128:160, tc * TOK:(tc + 1) * TOK],
                    in_=xbc_ev[:]), 16)

        @blk.gpsimd
        def _(gpsimd):
            gpsimd.wait_ge(s_xdb.h, s_xdb.c)
            cc.inc(gpsimd.collective_compute(
                "AllReduce", ALU.add, replica_groups=RG,
                ins=[xdbl_part.opt()], outs=[xdbl.opt()]))
        CC_XD = cc.c

        barrier(extra=(s_xcz, s_xdb))
        stC.close()
        stABC.close()

        stD = ExitStack()
        dtraw_bf = sb("dtraw_bf", [DTR, T], BF16, stD)
        bt_t = [sb(f"bt{i}", [128, DS * TC], F32, stD) for i in range(2)]
        ct_t = [sb(f"ct{i}", [128, DS * TC], F32, stD) for i in range(2)]
        g_t = [sb(f"gt{i}", [128, DS * TC], F32, stD) for i in range(3)]
        u_sb = [sb(f"u_sb{i}", [128, DS * TC], F32, stD) for i in range(2)]
        h_sb = sb("h_sb", [128, DS * TC], F32, stD)
        hst = sb("hst", [128, DS * NG], F32, stD)
        tmp16 = sb("tmp16", [128, DS], F32, stD)
        dt_t = [sb(f"dt{i}", [128, TC], F32, stD) for i in range(3)]
        w2_t = [sb(f"w2_{i}", [128, 2 * TC], F32, stD) for i in range(3)]
        tabs = sb("tabs", [128, TC], F32, stD)
        texp = sb("texp", [128, TC], F32, stD)
        tln = [sb(f"tln{i}", [128, TC], F32, stD) for i in range(2)]
        trelu = [sb(f"trelu{i}", [128, TC], F32, stD) for i in range(2)]
        xcg_t = [sb(f"xcg{i}", [128, TC], F32, stD) for i in range(4)]
        zg_t = [sb(f"zg{i}", [128, TC], F32, stD) for i in range(4)]
        y_sb = sb("y_sb", [128, TC], F32, stD)
        y2b_t = [sb(f"y2b{i}", [128, TC], BF16, stD) for i in range(4)]
        ps_dt = [psum(f"ps_dt{i}", [128, TC], stD) for i in range(3)]

        s_dst = [sem() for _ in range(2)]
        s_xcg = [sem() for _ in range(4)]
        s_zg = [sem() for _ in range(4)]
        s_bc = [sem() for _ in range(2)]
        s_y2 = [sem() for _ in range(4)]

        pp = sem()
        d0_cv = cv.c
        d0_cs = cs.c
        d0_tn = tn.c
        VI = 17
        PI = 1
        SI = 5
        SPL = 10

        @blk.sync
        def _(sync):
            sync.wait_ge(cc.h, CC_XD)
            for j in range(4):
                if j >= 2:
                    sync.wait_ge(cv.h, d0_cv + (j - 2) + 1)
                s_dst[j % 2].inc(sync.dma_start(
                    out=bt_t[0][:, (j % 2) * 2048:(j % 2 + 1) * 2048],
                    in_=xdbl[0:128, j * 2048:(j + 1) * 2048]), 16)

        @blk.vector
        def _(vector):
            for j in range(4):
                vector.wait_ge(s_dst[j % 2].h, 16 * (j // 2 + 1))
                vector.tensor_copy(out=dtraw_bf[:, j * 2048:(j + 1) * 2048],
                                   in_=bt_t[0][:, (j % 2) * 2048:(j % 2 + 1) * 2048]
                                   ).then_inc(cv.h, 1)
            vector.wait_ge(cv.h, d0_cv + 4)
            vector.memset(hst[:], 0.0).then_inc(cv.h, 1)
        cv.c += 5

        d1_cv = cv.c

        HEAD = ("dtadd", "wm", "wc")
        TAIL = ("ub", "foldm", "folda", "memset", "scan", "hst", "rmult",
                "tree1", "tree2", "tree3", "comb", "yfold", "stt", "y2b")
        sched = []
        for u in range(NU):
            sched.extend((op, u) for op in HEAD)
            if u >= 2:
                sched.extend((op, u - 2) for op in TAIL)
        sched.extend((op, NU - 2) for op in TAIL)
        sched.extend((op, NU - 1) for op in TAIL)
        L = {item: d1_cv + i + 1 for i, item in enumerate(sched)}

        def P(op, u):
            return u + 1

        @blk.sync
        def _(sync):
            sync.wait_ge(cc.h, CC_XD)
            for ch in range(NCH):
                c0 = ch * TC
                if ch == 0:
                    sync.wait_ge(cv.h, d0_cv + 4)
                if ch >= 2:
                    lastu = NG * (ch - 2) + (NG - 1)
                    sync.wait_ge(cv.h, L[("rmult", lastu)])
                for s in range(DS):
                    s_bc[ch % 2].inc(sync.dma_start(
                        out=bt_t[ch % 2][:, s * TC:(s + 1) * TC],
                        in_=xdbl[128 + s:129 + s, c0:c0 + TC]
                        .partition_broadcast(128)), 16)
                    s_bc[ch % 2].inc(sync.dma_start(
                        out=ct_t[ch % 2][:, s * TC:(s + 1) * TC],
                        in_=xdbl[144 + s:145 + s, c0:c0 + TC]
                        .partition_broadcast(128)), 16)
                for g in range(NG):
                    u = ch * NG + g
                    if u >= 4:
                        sync.wait_ge(cv.h, L[("stt", u - 4)])
                    s_xcg[u % 4].inc(sync.dma_start(
                        out=xcg_t[u % 4][:],
                        in_=xc_sp[g * 128:(g + 1) * 128,
                                  ch * TC:(ch + 1) * TC]), 16)
                    if u >= 4:
                        sync.wait_ge(cv.h, L[("y2b", u - 4)])
                    s_zg[u % 4].inc(sync.dma_start(
                        out=zg_t[u % 4][:],
                        in_=z_sp[g * 128:(g + 1) * 128,
                                 ch * TC:(ch + 1) * TC]), 16)

        @blk.gpsimd
        def _(gpsimd):
            gpsimd.wait_ge(cc.h, CC_XD)
            for ch in range(NCH):
                for g in range(NG):
                    u = ch * NG + g
                    if ch >= 1:
                        up = (ch - 1) * NG + g
                        qq, qc = (ch - 1) // 8, (ch - 1) % 8
                        gpsimd.wait_ge(cv.h, L[("y2b", up)])
                        s_y2[g].inc(gpsimd.dma_start(
                            out=y2_own[qq][g * 128:(g + 1) * 128,
                                          qc * TC:(qc + 1) * TC],
                            in_=y2b_t[g][:]), 16)
        pp.c += 0

        @blk.gpsimd
        def _(gpsimd):
            for g in range(NG):
                up = (NCH - 1) * NG + g
                gpsimd.wait_ge(cv.h, L[("y2b", up)])
                s_y2[g].inc(gpsimd.dma_start(
                    out=y2_own[3][g * 128:(g + 1) * 128, 7 * TC:],
                    in_=y2b_t[g][:]), 16)

        @blk.tensor
        def _(tensor):
            tensor.wait_ge(cv.h, d0_cv + 4)
            for ch in range(NCH):
                for g in range(NG):
                    u = ch * NG + g
                    if u >= 3:
                        tensor.wait_ge(cs.h, d0_cs + SI * (u - 3) + 4)
                    tn.inc(tensor.matmul(out=ps_dt[u % 3][:],
                                         lhsT=dtp_bf[:, g * 128:(g + 1) * 128],
                                         rhs=dtraw_bf[:, ch * TC:(ch + 1) * TC],
                                         start=True, stop=True))

        @blk.scalar
        def _(scalar):
            scalar.wait_ge(s_par.h, s_par.c)
            for ch in range(NCH):
                for g in range(NG):
                    u = ch * NG + g
                    csb = d0_cs + SI * u
                    scalar.wait_ge(tn.h, d0_tn + u + 1)
                    scalar.activation(out=tabs[:], in_=ps_dt[u % 3][:],
                                      func=AF.Abs,
                                      bias=dtb_s[:, g:g + 1]).then_inc(cs.h, 1)
                    scalar.wait_ge(cs.h, csb + 1)
                    scalar.activation(out=texp[:], in_=tabs[:], func=AF.Exp,
                                      scale=-1.0).then_inc(cs.h, 1)
                    scalar.wait_ge(cs.h, csb + 2)
                    if u >= 2:
                        scalar.wait_ge(cv.h, L[("dtadd", u - 2)])
                    scalar.activation(out=tln[u % 2][:], in_=texp[:], func=AF.Ln,
                                      bias=1.0).then_inc(cs.h, 1)
                    scalar.activation(out=trelu[u % 2][:], in_=ps_dt[u % 3][:],
                                      func=AF.Relu,
                                      bias=dtb_s[:, g:g + 1]).then_inc(cs.h, 1)
                    scalar.wait_ge(cv.h, L[("dtadd", u)])
                    if u >= 3:
                        scalar.wait_ge(cv.h, L[("scan", u - 3)])
                    ins = None
                    for s in range(DS):
                        ins = scalar.activation(
                            out=g_t[u % 3][:, s * TC:(s + 1) * TC],
                            in_=dt_t[u % 3][:], func=AF.Exp,
                            scale=a_s[:, g * DS + s:g * DS + s + 1])
                    ins.then_inc(cs.h, 1)
        cs.c += SI * NU

        def emit_head(vector, u):
            csb = d0_cs + SI * u
            vector.wait_ge(cs.h, csb + 4)
            if u >= 3:
                vector.wait_ge(cs.h, d0_cs + SI * (u - 3) + 5)
                vector.wait_ge(cv.h, L[("wc", u - 3)])
            vector.tensor_add(out=dt_t[u % 3][:], in0=tln[u % 2][:],
                              in1=trelu[u % 2][:]).then_inc(cv.h, 1)
            vector.wait_ge(s_xcg[u % 4].h, 16 * (u // 4 + 1))
            vector.wait_ge(cv.h, L[("dtadd", u)])
            if u >= 3:
                vector.wait_ge(cv.h, L[("ub", u - 3)])
            vector.tensor_mul(out=w2_t[u % 3][:, 0:TC], in0=dt_t[u % 3][:],
                              in1=xcg_t[u % 4][:]).then_inc(cv.h, 1)
            vector.wait_ge(cv.h, L[("wm", u)])
            vector.tensor_copy(out=w2_t[u % 3][:, TC:2 * TC],
                               in_=w2_t[u % 3][:, 0:TC]).then_inc(cv.h, 1)

        def emit_tail(vector, u):
            ch, g = u // NG, u % NG
            csb = d0_cs + SI * u
            vector.wait_ge(s_bc[ch % 2].h, 512 * (ch // 2 + 1))
            vector.wait_ge(cv.h, L[("wc", u)])
            ins = None
            for s2 in range(0, DS, 2):
                ins = vector.tensor_mul(
                    out=u_sb[u % 2][:, s2 * TC:(s2 + 2) * TC],
                    in0=w2_t[u % 3][:],
                    in1=bt_t[ch % 2][:, s2 * TC:(s2 + 2) * TC])
            ins.then_inc(cv.h, 1)
            vector.wait_ge(cs.h, csb + 5)
            vector.wait_ge(cv.h, L[("ub", u)])
            if u >= 4:
                vector.wait_ge(cv.h, L[("hst", u - 4)])
            gst = g_t[u % 3][:, 0::TC]
            ust = u_sb[u % 2][:, 0::TC]
            vector.tensor_mul(out=tmp16[:], in0=gst,
                              in1=hst[:, g * DS:(g + 1) * DS]).then_inc(cv.h, 1)
            vector.wait_ge(cv.h, L[("foldm", u)])
            vector.tensor_add(out=ust, in0=ust, in1=tmp16[:]).then_inc(cv.h, 1)
            vector.wait_ge(cv.h, L[("folda", u)])
            vector.memset(gst, 0.0).then_inc(cv.h, 1)
            vector.wait_ge(cv.h, L[("memset", u)])
            vector.tensor_tensor_scan(
                out=h_sb[:], data0=g_t[u % 3][:], data1=u_sb[u % 2][:],
                initial=0.0, op0=ALU.mult, op1=ALU.add).then_inc(cv.h, 1)
            vector.wait_ge(cv.h, L[("scan", u)])
            vector.tensor_copy(out=hst[:, g * DS:(g + 1) * DS],
                               in_=h_sb[:, TC - 1::TC]).then_inc(cv.h, 1)
            vector.wait_ge(cv.h, L[("hst", u)])
            ins = None
            for s2 in range(0, DS, 2):
                ins = vector.tensor_mul(
                    out=h_sb[:, s2 * TC:(s2 + 2) * TC],
                    in0=h_sb[:, s2 * TC:(s2 + 2) * TC],
                    in1=ct_t[ch % 2][:, s2 * TC:(s2 + 2) * TC])
            ins.then_inc(cv.h, 1)
            vector.wait_ge(cv.h, L[("rmult", u)])
            vector.tensor_add(out=h_sb[:, 0:2 * TC], in0=h_sb[:, 0:2 * TC],
                              in1=h_sb[:, 2 * TC:4 * TC])
            vector.tensor_add(out=h_sb[:, 4 * TC:6 * TC],
                              in0=h_sb[:, 4 * TC:6 * TC],
                              in1=h_sb[:, 6 * TC:8 * TC]).then_inc(cv.h, 1)
            vector.wait_ge(cv.h, L[("tree1", u)])
            vector.tensor_add(out=h_sb[:, 8 * TC:10 * TC],
                              in0=h_sb[:, 8 * TC:10 * TC],
                              in1=h_sb[:, 10 * TC:12 * TC])
            vector.tensor_add(out=h_sb[:, 12 * TC:14 * TC],
                              in0=h_sb[:, 12 * TC:14 * TC],
                              in1=h_sb[:, 14 * TC:16 * TC]).then_inc(cv.h, 1)
            vector.wait_ge(cv.h, L[("tree2", u)])
            vector.tensor_add(out=h_sb[:, 0:2 * TC], in0=h_sb[:, 0:2 * TC],
                              in1=h_sb[:, 4 * TC:6 * TC])
            vector.tensor_add(out=h_sb[:, 8 * TC:10 * TC],
                              in0=h_sb[:, 8 * TC:10 * TC],
                              in1=h_sb[:, 12 * TC:14 * TC]).then_inc(cv.h, 1)
            vector.wait_ge(cv.h, L[("tree3", u)])
            vector.tensor_add(out=h_sb[:, 0:2 * TC], in0=h_sb[:, 0:2 * TC],
                              in1=h_sb[:, 8 * TC:10 * TC]).then_inc(cv.h, 1)
            vector.wait_ge(cv.h, L[("comb", u)])
            vector.tensor_add(out=y_sb[:], in0=h_sb[:, 0:TC],
                              in1=h_sb[:, TC:2 * TC]).then_inc(cv.h, 1)
            vector.wait_ge(cv.h, L[("yfold", u)])
            vector.scalar_tensor_tensor(
                out=y_sb[:], in0=xcg_t[u % 4][:],
                scalar=dskip_s[:, g:g + 1], in1=y_sb[:],
                op0=ALU.mult, op1=ALU.add).then_inc(cv.h, 1)
            vector.wait_ge(cv.h, L[("stt", u)])
            vector.wait_ge(s_zg[u % 4].h, 16 * (u // 4 + 1))
            if ch >= 1:
                vector.wait_ge(s_y2[g].h, 16 * ch)
            vector.tensor_mul(out=y2b_t[g][:], in0=y_sb[:],
                              in1=zg_t[u % 4][:]).then_inc(cv.h, 1)

        @blk.vector
        def _(vector):
            for u in range(NU):
                emit_head(vector, u)
                if u >= 2:
                    emit_tail(vector, u - 2)
            emit_tail(vector, NU - 2)
            emit_tail(vector, NU - 1)
        cv.c += VI * NU

        s_yl = sem()
        s_ag = [sem() for _ in range(4)]

        @blk.gpsimd
        def _(gpsimd):
            for q in range(4):
                for g in range(NG):
                    gpsimd.wait_ge(s_y2[g].h, 16 * 8 * (q + 1))
                s_ag[q].inc(gpsimd.collective_compute(
                    "AllGather", ALU.bypass, replica_groups=RG,
                    ins=[y2_own[q].opt()], outs=[y2_all[q].opt()]))
                cc.c += 0
        CC_Y2 = cc.c

        barrier(extra=(s_y2[0], s_y2[1], s_y2[2], s_y2[3]))
        stD.close()

        stE = ExitStack()
        rhs2 = sb("rhs2", [128, 32 * TOK], BF16, stE)
        wst_e = [sb(f"wste{i}", [128, 512], F32, stE) for i in range(4)]
        wbf_e = [sb(f"wbfe{i}", [128, 512], BF16, stE) for i in range(4)]
        u0t = [sb(f"u0t{i}", [128, TOK], F32, stE) for i in range(2)]
        v_sb = sb("v_sb", [128, NFC * TOK], F32)
        ps_o = [psum(f"ps_o{i}", [128, TOK], stE) for i in range(4)]

        s_r2 = sem()
        s_we = [sem() for _ in range(4)]
        ce = sem()
        s_u0t = [sem() for _ in range(2)]

        e0_tn = tn.c
        e0_cv = cv.c
        e0_cs = cs.c

        @blk.sync
        def _(sync):
            for j in range(4):
                s_we[j % 4].inc(sync.dma_start(
                    out=wst_e[j % 4][:],
                    in_=outpT[j * 128:(j + 1) * 128, 0:512]), 16)
            for arm in nc.switch_core_id(engines=[sync], n=128):
                rk = arm.logical % W
                sync.wait_ge(s_ag[rk // 2].h, 1)
                co = (rk % 2) * TOK
                for k in range(32):
                    sync.dma_start(
                        out=rhs2[:, k * TOK:(k + 1) * TOK],
                        in_=y2_all[rk // 2][k * 128:(k + 1) * 128,
                                            co:co + TOK]).then_inc(s_r2.h, 16)
            s_r2.c += 16 * 32
            for mq in range(4):
                for k in range(32):
                    j = mq * 32 + k
                    if j < 4:
                        continue
                    sync.wait_ge(ce.h, (j - 4) + 1)
                    s_we[j % 4].inc(sync.dma_start(
                        out=wst_e[j % 4][:],
                        in_=outpT[k * 128:(k + 1) * 128,
                                  mq * 512:(mq + 1) * 512]), 16)

        @blk.scalar
        def _(scalar):
            for j in range(128):
                scalar.wait_ge(s_we[j % 4].h, 16 * (j // 4 + 1))
                if j >= 4:
                    scalar.wait_ge(tn.h, e0_tn + 4 * (j - 3))
                scalar.copy(out=wbf_e[j % 4][:],
                            in_=wst_e[j % 4][:]).then_inc(ce.h, 1)
                if j % 32 == 31:
                    mq = j // 32
                    scalar.wait_ge(tn.h, e0_tn + 128 * (mq + 1))
                    for m in range(4):
                        fc = mq * 4 + m
                        cs.inc(scalar.copy(
                            out=v_sb[:, fc * TOK:(fc + 1) * TOK],
                            in_=ps_o[m][:]))
        ce.c += 128

        @blk.tensor
        def _(tensor):
            tensor.wait_ge(s_r2.h, s_r2.c)
            for mq in range(4):
                if mq >= 1:
                    tensor.wait_ge(cs.h, e0_cs + 4 * mq)
                for k in range(32):
                    j = mq * 32 + k
                    tensor.wait_ge(ce.h, j + 1)
                    ins = None
                    for m in range(4):
                        ins = mm512(tensor, ps_o[m],
                                    wbf_e[j % 4][:, m * 128:(m + 1) * 128],
                                    rhs2[:, k * TOK:(k + 1) * TOK],
                                    start=(k == 0), stop=(k == 31))
                    tn.inc(ins, 4)

        e1_cv = cv.c

        @blk.sync
        def _(sync):
            for fc in range(NFC):
                if fc >= 2:
                    sync.wait_ge(cv.h, e1_cv + (fc - 2) + 1)
                s_u0t[fc % 2].inc(sync.dma_start(
                    out=u0t[fc % 2][:],
                    in_=u0_sp[fc * 128:(fc + 1) * 128, :]), 16)

        @blk.vector
        def _(vector):
            for fc in range(NFC):
                vector.wait_ge(cs.h, e0_cs + fc + 1)
                vector.wait_ge(s_u0t[fc % 2].h, 16 * (fc // 2 + 1))
                vector.tensor_add(out=v_sb[:, fc * TOK:(fc + 1) * TOK],
                                  in0=v_sb[:, fc * TOK:(fc + 1) * TOK],
                                  in1=u0t[fc % 2][:]).then_inc(cv.h, 1)
        cv.c += NFC

        barrier()
        stE.close()

        stG = ExitStack()
        ffh_sb = sb("ffh_sb", [128, 32 * TOK], BF16, stG)
        a2b_sb = sb("a2b_sb", [128, NFC * TOK], BF16, stG)
        stF = ExitStack()
        pstF = psum("pstF", [128, TOK], stF)
        psrF = psum("psrF", [128, TOK], stF)
        psqF = psum("psqF", [128, TOK], stF)
        scrF = (sb("sq0F", [128, TOK], F32, stF), sb("sq1F", [128, TOK], F32, stF),
                sb("mF", [1, TOK], F32, stF), sb("rF", [1, TOK], F32, stF),
                sb("qF", [1, TOK], F32, stF), sb("t1F", [128, TOK], F32, stF))
        emit_ln(lambda fc: v_sb[:, fc * TOK:(fc + 1) * TOK],
                lambda fc: a2b_sb[:, fc * TOK:(fc + 1) * TOK],
                ln2w_s, ln2b_s, (pstF, psrF, psqF), scrF,
                pre=((s_par, s_par.c),))
        barrier()
        stF.close()

        stG1 = ExitStack()
        wstg_p = [sb(f"wsp{i}", [128, NFC * 128], F32, stG1) for i in range(2)]
        wbf_a = [sb(f"wba{i}", [128, NFC * 128], BF16, stG1) for i in range(2)]
        wbf_g = [sb(f"wbg{i}", [128, NFC * 128], BF16, stG1) for i in range(2)]
        sg_sb = sb("sg_sb", [128, TOK], F32, stG1)
        ps_a = psum("ps_a", [128, TOK], stG1)
        ps_g = psum("ps_g", [128, TOK], stG1)

        s_wa = [sem() for _ in range(2)]
        s_wg = [sem() for _ in range(2)]
        ca = sem()
        g0_tn = tn.c
        g0_cs = cs.c
        g0_cv = cv.c

        @blk.sync
        def _(sync):
            for hc in range(32):
                if hc >= 1:
                    sync.wait_ge(ca.h, 2 * (hc - 1) + 1)
                s_wa[0].inc(sync.dma_start(
                    out=wstg_p[0][:].rearrange("p (k m) -> p k m", m=128),
                    in_=ffpT[:, hc * 128:(hc + 1) * 128].rearrange(
                        "(k p) m -> p k m", p=128)), 16)
                if hc >= 1:
                    sync.wait_ge(ca.h, 2 * (hc - 1) + 2)
                s_wg[0].inc(sync.dma_start(
                    out=wstg_p[1][:].rearrange("p (k m) -> p k m", m=128),
                    in_=ffpT[:, HH + hc * 128:HH + (hc + 1) * 128].rearrange(
                        "(k p) m -> p k m", p=128)), 16)

        @blk.scalar
        def _(scalar):
            for hc in range(33):
                if hc < 32:
                    scalar.wait_ge(s_wa[0].h, 16 * (hc + 1))
                    if hc >= 2:
                        scalar.wait_ge(tn.h, g0_tn + 2 * (hc - 1))
                    scalar.copy(out=wbf_a[hc % 2][:],
                                in_=wstg_p[0][:]).then_inc(ca.h, 1)
                    scalar.wait_ge(s_wg[0].h, 16 * (hc + 1))
                    scalar.copy(out=wbf_g[hc % 2][:],
                                in_=wstg_p[1][:]).then_inc(ca.h, 1)
                if hc >= 1:
                    h = hc - 1
                    scalar.wait_ge(tn.h, g0_tn + 2 * h + 2)
                    if h >= 1:
                        scalar.wait_ge(cv.h, g0_cv + h)
                    cs.inc(scalar.activation(out=sg_sb[:], in_=ps_g[:],
                                             func=AF.Silu))
        ca.c += 64

        @blk.vector
        def _(vector):
            for hc in range(32):
                vector.wait_ge(cs.h, g0_cs + hc + 1)
                vector.tensor_mul(out=ffh_sb[:, hc * TOK:(hc + 1) * TOK],
                                  in0=ps_a[:], in1=sg_sb[:]).then_inc(cv.h, 1)
        cv.c += 32

        @blk.tensor
        def _(tensor):
            for hc in range(32):
                tensor.wait_ge(ca.h, 2 * hc + 2)
                if hc >= 1:
                    tensor.wait_ge(cv.h, g0_cv + hc)
                for k in range(NFC):
                    mm512(tensor, ps_a, wbf_a[hc % 2][:, k * 128:(k + 1) * 128],
                          a2b_sb[:, k * TOK:(k + 1) * TOK],
                          start=(k == 0), stop=(k == NFC - 1))
                ins = None
                for k in range(NFC):
                    ins = mm512(tensor, ps_g,
                                wbf_g[hc % 2][:, k * 128:(k + 1) * 128],
                                a2b_sb[:, k * TOK:(k + 1) * TOK],
                                start=(k == 0), stop=(k == NFC - 1))
                tn.inc(ins, 2)

        barrier()
        stG1.close()

        stG2 = ExitStack()
        ffo_t = [sb(f"ffot{i}", [128, TOK], F32, stG2) for i in range(2)]
        xr_t = [sb(f"xrt{i}", [128, TOK], F32, stG2) for i in range(2)]
        wst_o = [sb(f"wsto{i}", [128, 512], F32, stG2) for i in range(4)]
        wbf_o = [sb(f"wbfo{i}", [128, 512], BF16, stG2) for i in range(4)]
        ps_f = [psum(f"ps_f{i}", [128, TOK], stG2) for i in range(4)]

        s_wo = [sem() for _ in range(4)]
        s_xr = [sem() for _ in range(2)]
        po = sem()
        g1_tn = tn.c
        g1_cv = cv.c
        g1_cs = cs.c

        @blk.sync
        def _(sync):
            for mq in range(4):
                for k in range(32):
                    j = mq * 32 + k
                    if j >= 4:
                        sync.wait_ge(cv.h, g1_cv + (j - 4) + 1)
                    s_wo[j % 4].inc(sync.dma_start(
                        out=wst_o[j % 4][:],
                        in_=ffoT[k * 128:(k + 1) * 128,
                                 mq * 512:(mq + 1) * 512]), 16)

        @blk.scalar
        def _(scalar):
            for j in range(128):
                scalar.wait_ge(s_wo[j % 4].h, 16 * (j // 4 + 1))
                if j >= 4:
                    scalar.wait_ge(tn.h, g1_tn + 4 * (j - 3))
                scalar.copy(out=wbf_o[j % 4][:],
                            in_=wst_o[j % 4][:]).then_inc(cv.h, 1)
                if j % 32 == 31:
                    mq = j // 32
                    scalar.wait_ge(tn.h, g1_tn + 128 * (mq + 1))
                    for m in range(4):
                        fc = mq * 4 + m
                        if fc >= 2:
                            scalar.wait_ge(po.h, 2 * (fc - 2) + 1)
                        cs.inc(scalar.copy(out=ffo_t[fc % 2][:],
                                           in_=ps_f[m][:]))
        cv.c += 128

        @blk.tensor
        def _(tensor):
            for mq in range(4):
                if mq >= 1:
                    tensor.wait_ge(cs.h, g1_cs + 4 * mq)
                for k in range(32):
                    j = mq * 32 + k
                    tensor.wait_ge(cv.h, g1_cv + j + 1)
                    ins = None
                    for m in range(4):
                        ins = mm512(tensor, ps_f[m],
                                    wbf_o[j % 4][:, m * 128:(m + 1) * 128],
                                    ffh_sb[:, k * TOK:(k + 1) * TOK],
                                    start=(k == 0), stop=(k == 31))
                    tn.inc(ins, 4)

        @blk.gpsimd
        def _(gpsimd):
            for fc in range(NFC):
                if fc >= 2:
                    gpsimd.wait_ge(po.h, 2 * (fc - 2) + 2)
                s_xr[fc % 2].inc(gpsimd.dma_start(
                    out=xr_t[fc % 2][:],
                    in_=xT[fc * 128:(fc + 1) * 128, :]), 16)
                gpsimd.wait_ge(cs.h, g1_cs + fc + 1)
                gpsimd.tensor_add(out=v_sb[:, fc * TOK:(fc + 1) * TOK],
                                  in0=v_sb[:, fc * TOK:(fc + 1) * TOK],
                                  in1=ffo_t[fc % 2][:]).then_inc(po.h, 1)
                gpsimd.wait_ge(po.h, 2 * fc + 1)
                gpsimd.wait_ge(s_xr[fc % 2].h, 16 * (fc // 2 + 1))
                gpsimd.tensor_add(out=v_sb[:, fc * TOK:(fc + 1) * TOK],
                                  in0=v_sb[:, fc * TOK:(fc + 1) * TOK],
                                  in1=xr_t[fc % 2][:]).then_inc(po.h, 1)
                gpsimd.wait_ge(po.h, 2 * fc + 2)
        po.c += 2 * NFC
        x1_sb = v_sb

        barrier()
        stG2.close()
        stG.close()

        stH = ExitStack()
        u1b_sb = sb("u1b_sb", [128, NFC * TOK], BF16)
        pstH = psum("pstH", [128, TOK], stH)
        psrH = psum("psrH", [128, TOK], stH)
        psqH = psum("psqH", [128, TOK], stH)
        scrH = (sb("sq0H", [128, TOK], F32, stH), sb("sq1H", [128, TOK], F32, stH),
                sb("mH", [1, TOK], F32, stH), sb("rH", [1, TOK], F32, stH),
                sb("qH", [1, TOK], F32, stH), sb("t1H", [128, TOK], F32, stH))
        emit_ln(lambda fc: x1_sb[:, fc * TOK:(fc + 1) * TOK],
                lambda fc: u1b_sb[:, fc * TOK:(fc + 1) * TOK],
                None, None, (pstH, psrH, psqH), scrH)
        barrier()
        stH.close()

        stI = ExitStack()
        wbf_i = [sb(f"wbfi{i}", [128, 512], BF16, stI) for i in range(4)]
        out_t = [sb(f"outt{i}", [128, TOK], F32, stI) for i in range(2)]
        ps_i = [psum(f"ps_i{i}", [128, TOK], stI) for i in range(4)]

        s_wi = [sem() for _ in range(4)]
        so_p = [sem() for _ in range(2)]
        i0_tn = tn.c
        i0_cs = cs.c
        i0_cv = cv.c

        @blk.sync
        def _(sync):
            sync.wait_ge(cc.h, CC_FS)
            for mq in range(4):
                for k in range(NFC):
                    j = mq * NFC + k
                    if j >= 4:
                        sync.wait_ge(tn.h, i0_tn + 4 * (j - 3))
                        sync.wait_ge(s_wi[j % 4].h, 64 * ((j - 4) // 4 + 1))
                    ins = None
                    for m in range(4):
                        e = mq * 4 + m
                        r = e // 2
                        col = (e % 2) * 128
                        ins = sync.dma_start(
                            out=wbf_i[j % 4][:, m * 128:(m + 1) * 128],
                            in_=fs_all[r * D + k * 128:r * D + (k + 1) * 128,
                                       col:col + 128])
                        s_wi[j % 4].inc(ins, 16)

        @blk.tensor
        def _(tensor):
            for mq in range(4):
                if mq >= 1:
                    tensor.wait_ge(cs.h, i0_cs + 4 * mq)
                for k in range(NFC):
                    j = mq * NFC + k
                    tensor.wait_ge(s_wi[j % 4].h, 64 * (j // 4 + 1))
                    ins = None
                    for m in range(4):
                        ins = mm512(tensor, ps_i[m],
                                    wbf_i[j % 4][:, m * 128:(m + 1) * 128],
                                    u1b_sb[:, k * TOK:(k + 1) * TOK],
                                    start=(k == 0), stop=(k == NFC - 1))
                    tn.inc(ins, 4)

        @blk.scalar
        def _(scalar):
            scalar.wait_ge(s_par.h, s_par.c)
            for mq in range(4):
                for m in range(4):
                    fc = mq * 4 + m
                    scalar.wait_ge(tn.h, i0_tn + 64 * (mq + 1))
                    if fc >= 2:
                        scalar.wait_ge(so_p[fc % 2].h, 16 * (fc // 2))
                    cs.inc(scalar.activation(
                        out=out_t[fc % 2][:], in_=ps_i[m][:], func=AF.Identity,
                        scale=RSQRT8, bias=fbias_s[:, fc:fc + 1]))

        @blk.vector
        def _(vector):
            for fc in range(NFC):
                vector.wait_ge(cs.h, i0_cs + fc + 1)
                vector.tensor_add(out=out_t[fc % 2][:],
                                  in0=out_t[fc % 2][:],
                                  in1=x1_sb[:, fc * TOK:(fc + 1) * TOK]
                                  ).then_inc(cv.h, 1)
        cv.c += NFC

        @blk.gpsimd
        def _(gpsimd):
            for fc in range(NFC):
                gpsimd.wait_ge(cv.h, i0_cv + fc + 1)
                so_p[fc % 2].inc(gpsimd.dma_start(
                    out=out_fm[fc * 128:(fc + 1) * 128, :],
                    in_=out_t[fc % 2][:]), 16)
            gpsimd.wait_ge(so_p[0].h, so_p[0].c)
            gpsimd.wait_ge(so_p[1].h, so_p[1].c)

        barrier()
        stI.close()

    st.close()
    return nc


_NC_CACHE = {}


def kernel(**inputs):
    f32 = np.float32

    def cac(a):
        return np.ascontiguousarray(a, dtype=f32)

    x = np.asarray(inputs["L0_attn_x"])[0]
    xT = cac(x.T)
    w_in = np.asarray(inputs["L0_attn_in_proj_w"])
    wInXT = cac(w_in[:DI].T)
    wInZT = cac(w_in[DI:].T)
    conv_w = np.asarray(inputs["L0_attn_conv_w"])[:, 0, :]
    conv_b = np.asarray(inputs["L0_attn_conv_b"])
    xpT_f = cac(np.asarray(inputs["L0_attn_x_proj_w"]).T)
    dtpT_f = cac(np.asarray(inputs["L0_attn_dt_proj_w"]).T)
    dt_b = np.asarray(inputs["L0_attn_dt_proj_b"])
    A_log = np.asarray(inputs["L0_attn_A_log"])
    Dskip = np.asarray(inputs["L0_attn_D_skip"])
    outpT = cac(np.asarray(inputs["L0_attn_out_proj_w"]).T)
    ffpT = cac(np.asarray(inputs["L0_attn_ff_proj_w"]).T)
    ffoT = cac(np.asarray(inputs["L0_attn_ff_out_w"]).T)
    factors = np.asarray(inputs["L0_ffn_factors"])

    def col16(vec):
        return cac(np.asarray(vec).reshape(NFC, 128).T)

    def colg(vec):
        return cac(np.asarray(vec).reshape(NG, 128).T)

    ln1w = col16(inputs["L0_attn_ln1_w"]); ln1b = col16(inputs["L0_attn_ln1_b"])
    ln2w = col16(inputs["L0_attn_ln2_w"]); ln2b = col16(inputs["L0_attn_ln2_b"])
    fbias_t = col16(inputs["L0_ffn_bias"])

    in_maps = []
    for c in range(W):
        dsl = slice(c * DIC, (c + 1) * DIC)
        in_maps.append({
            "xT": cac(xT[:, c * TOK:(c + 1) * TOK]),
            "ln1w": ln1w, "ln1b": ln1b, "ln2w": ln2w, "ln2b": ln2b,
            "fbias": fbias_t,
            "wInXT": cac(wInXT[:, dsl]),
            "wInZT": cac(wInZT[:, dsl]),
            "convw": cac(conv_w[dsl].reshape(NG, 128, 4).transpose(1, 0, 2)
                         .reshape(128, 4 * NG)),
            "convb": colg(conv_b[dsl]),
            "dskip": colg(Dskip[dsl]),
            "dtb": colg(dt_b[dsl]),
            "alog": cac(A_log[dsl].reshape(NG, 128, DS).transpose(1, 0, 2)
                        .reshape(128, DS * NG)),
            "xpT": cac(xpT_f[dsl].reshape(NG, 128, 160).transpose(1, 0, 2)
                       .reshape(128, 160 * NG)),
            "dtpT": cac(dtpT_f[:, dsl]),
            "outpT": outpT,
            "ffpT": ffpT,
            "ffoT": ffoT,
            "fct": cac(factors[:, :, c * (D // W):(c + 1) * (D // W)]
                       .reshape(8 * D, D // W)),
            "consts": np.tile(np.array([[EPS, 1.0, -1.0, RSQRT8]], f32),
                              (128, 1)),
        })

    if "nc" not in _NC_CACHE:
        _NC_CACHE["nc"] = build_nc()
    nc = _NC_CACHE["nc"]
    res = run_bass_kernel_spmd(nc, in_maps, core_ids=list(range(W)))
    out = np.concatenate(
        [np.asarray(res.results[c]["out_fm"]).T for c in range(W)], axis=0)
    return np.ascontiguousarray(out[None], dtype=np.float32)
